# version 16
# baseline (speedup 1.0000x reference)
import numpy as np
import ml_dtypes
from contextlib import ExitStack

import concourse.bass as bass
import concourse.mybir as mybir
from concourse.bass_utils import run_bass_kernel_spmd

F32 = mybir.dt.float32
BF16 = mybir.dt.bfloat16
AF = mybir.ActivationFunctionType
ALU = mybir.AluOpType
AX = mybir.AxisListType
BFNP = ml_dtypes.bfloat16

D = 1024
HID = 4096
SEQ = 2048
NSEQ = 2
T = NSEQ * SEQ
NCH = T // 128
TSB = 512
NSB = T // TSB
CPS = TSB // 128
NH = 16
HD = 64
NPAIR = 8
EPS = 1e-5
EXP_SHIFT = -16.0

RUN_PHASES = ("f1", "attn", "f2")
DEBUG_TAPS = False
DEV_NSB = None
DEV_SKIP = ()
DEV_RSTD_AFTER = "ci1"


class Sem:
    def __init__(self, es, nc, name):
        self.h = es.enter_context(nc.semaphore(name))
        self.n = 0

    def bump(self, k=1):
        self.n += k
        return self.n


class Eng:
    def __init__(self, name):
        self.name = name
        self.ops = []

    def __call__(self, fn):
        self.ops.append(fn)

    def wait(self, sem, val):
        if val <= 0:
            return
        assert val <= sem.n, f"{self.name} waits future {val} > {sem.n}"
        self.ops.append(lambda e, h=sem.h, v=val: e.wait_ge(h, v))


def build_program():
    nc = bass.Bass("TRN2", target_bir_lowering=False)
    es = ExitStack()

    dma = Sem(es, nc, "s_dma")
    dmr = Sem(es, nc, "s_dmr")
    sx = [Sem(es, nc, f"s_x{i}") for i in range(2)]
    sxr = [Sem(es, nc, f"s_xr{i}") for i in range(2)]
    sw = [Sem(es, nc, f"s_w{i}") for i in range(3)]
    dmw = Sem(es, nc, "s_dmw")
    dmc = Sem(es, nc, "s_dmc")
    dmo = Sem(es, nc, "s_dmo")
    dmg = Sem(es, nc, "s_dmg")
    pe = Sem(es, nc, "s_pe")
    act = Sem(es, nc, "s_act")
    dve = Sem(es, nc, "s_dve")
    pool = Sem(es, nc, "s_pool")

    SY = Eng("sync"); PE = Eng("tensor"); ACT = Eng("scalar")
    DVE = Eng("vector"); PO = Eng("gpsimd")

    def flush_block():
        with nc.Block() as block:
            for attr, eng in (("sync", SY), ("tensor", PE), ("scalar", ACT),
                              ("vector", DVE), ("gpsimd", PO)):
                if not eng.ops:
                    continue
                ops, eng.ops = eng.ops, []

                def body(e, ops=ops):
                    for fn in ops:
                        fn(e)
                getattr(block, attr)(body)

    x_in = nc.dram_tensor("x_in", [T, D], F32, kind="ExternalInput").ap()
    wfcT_h = nc.dram_tensor("wfcT", [D, HID], F32, kind="ExternalInput").ap()
    wpT_h = nc.dram_tensor("wpT", [HID, D], F32, kind="ExternalInput").ap()
    wqk_h = nc.dram_tensor("wqk", [D, 2 * D], F32, kind="ExternalInput").ap()
    wv_h = nc.dram_tensor("wv", [D, D], F32, kind="ExternalInput").ap()
    bqk_pt_h = nc.dram_tensor("bqk_pt", [128, 16], F32, kind="ExternalInput").ap()
    lnw_pt_h = nc.dram_tensor("lnw_pt", [128, 32], F32, kind="ExternalInput").ap()
    lnb_pt_h = nc.dram_tensor("lnb_pt", [128, 32], F32, kind="ExternalInput").ap()
    bv_t_h = nc.dram_tensor("bv_t", [128, D], F32, kind="ExternalInput").ap()
    id_bf_h = nc.dram_tensor("id_bf", [128, 128], BF16, kind="ExternalInput").ap()
    id_f32_h = nc.dram_tensor("id_f32", [128, 128], F32, kind="ExternalInput").ap()
    ones_col_h = nc.dram_tensor("ones_col", [128, 1], BF16, kind="ExternalInput").ap()
    ones_row_h = nc.dram_tensor("ones_row", [1, 128], F32, kind="ExternalInput").ap()
    mask_ut_h = nc.dram_tensor("mask_ut", [128, 128], BF16, kind="ExternalInput").ap()
    cst_h = nc.dram_tensor("cst", [128, 8], F32, kind="ExternalInput").ap()

    y_out = nc.dram_tensor("y", [T, D], F32, kind="ExternalOutput").ap()
    xs1 = nc.dram_tensor("xs1", [T, D], F32).ap()
    xs2 = nc.dram_tensor("xs2", [T, D], F32).ap()
    wspill = nc.dram_tensor("wspill", [128, 2 * 32768], BF16).ap()
    if DEBUG_TAPS:
        dbg_uT = nc.dram_tensor("dbg_uT", [128, 8 * TSB], BF16, kind="ExternalOutput").ap()
        dbg_gT = nc.dram_tensor("dbg_gT", [128, 32 * TSB], BF16, kind="ExternalOutput").ap()
        dbg_rows = nc.dram_tensor("dbg_rows", [128, 16], F32, kind="ExternalOutput").ap()
        dbg_w = nc.dram_tensor("dbg_w", [128, 2048], BF16, kind="ExternalOutput").ap()
        dbg_u = nc.dram_tensor("dbg_u", [128, 1024], BF16, kind="ExternalOutput").ap()
        dbg_st = nc.dram_tensor("dbg_st", [128, 10], F32, kind="ExternalOutput").ap()
        dbg_st2 = nc.dram_tensor("dbg_st2", [128, 10], F32, kind="ExternalOutput").ap()
        dbg_gT2 = nc.dram_tensor("dbg_gT2", [128, 32 * TSB], BF16, kind="ExternalOutput").ap()
        dbg_wf = nc.dram_tensor("dbg_wf", [128, 32768], BF16, kind="ExternalOutput").ap()
        dbg_wp = nc.dram_tensor("dbg_wp", [128, 32768], BF16, kind="ExternalOutput").ap()

    def sb(name, shape, dt, stack=es):
        return stack.enter_context(nc.sbuf_tensor(name, shape, dt))

    xin = [sb(f"xin{i}", [128, D], F32) for i in range(2)]
    xout = sb("xout", [128, D], F32)
    u_t = sb("u_t", [128, D], BF16)
    stat = [sb(f"stat{i}", [128, 10], F32) for i in range(2)]
    rows = sb("rows", [1, 3 * TSB], F32)
    negmurow = sb("negmurow", [1, TSB], BF16)
    rstd05 = sb("rstd05", [128, CPS], F32)
    cst = sb("cstv", [128, 8], F32)
    id_bf = sb("id_bf_s", [128, 128], BF16)
    ones_col = sb("ones_col_s", [128, 1], BF16)
    ones_row = sb("ones_row_s", [1, 128], F32)
    bqk_pt = sb("bqk_pt_s", [128, 16], F32)
    lnw_pt = sb("lnw_pt_s", [128, 32], F32)
    lnb_pt = sb("lnb_pt_s", [128, 32], F32)
    lnb_bf = sb("lnb_bf", [128, 32], BF16)
    bv_t = sb("bv_t_s", [128, D], BF16)
    fbias_t = sb("fbias_t", [128, D], BF16)
    wsum_row = sb("wsum_row", [1, D], BF16)
    uT3 = sb("uT3", [128, 8, TSB], BF16)
    xsq = sb("xsq", [128, D], BF16)
    sums_t = sb("sums_t", [128, 2 * NCH], F32)
    rstd_t = sb("rstd_t", [128, NCH], F32)
    negmu_t = sb("negmu_t", [128, NCH], F32)
    lnrows = sb("lnrows", [128, 2 * NCH], F32)

    nc.const_aps.register(F32, EPS, cst[:, 0:1])
    nc.const_aps.register(F32, EXP_SHIFT, cst[:, 1:2])
    nc.const_aps.register(F32, 0.0, cst[:, 2:3])
    nc.const_aps.register(F32, 1.0, cst[:, 3:4])
    nc.const_aps.register(F32, float(np.log(0.5)), cst[:, 4:5])
    nc.const_aps.register(F32, 0.5e-6, cst[:, 5:6])

    tpA = es.enter_context(nc.psum_tensor("tpA", [128, 512], BF16))
    tpB = es.enter_context(nc.psum_tensor("tpB", [128, 512], BF16))
    psA = es.enter_context(nc.psum_tensor("psA", [128, 1024], F32))
    psB = es.enter_context(nc.psum_tensor("psB", [128, 1024], F32))
    psC = es.enter_context(nc.psum_tensor("psC", [128, 1024], F32))

    def dmain(eng, sem, out_ap, in_ap):
        eng(lambda e, o=out_ap, i=in_ap: e.dma_start(out=o, in_=i).then_inc(sem.h, 16))
        return sem.bump(16)

    def r2(ap):
        return ap.rearrange("p a b -> p (a b)")

    tp_free = [0, 0]
    st_gate = [0, 0]

    def emit_ln_chunk(src_hbm, g, cbuf, xin_gate_dve, ut_gate_pe, uT_gate_pe,
                      table_done):
        SY.wait(dve, xin_gate_dve)
        vin = dmain(SY, sx[cbuf], xin[cbuf][:, :], src_hbm[128 * g:128 * (g + 1), :])
        DVE.wait(sx[cbuf], vin)
        DVE.wait(pe, ut_gate_pe)
        DVE.wait(act, table_done[0])
        DVE(lambda e, g=g: e.tensor_scalar(
            out=u_t[:, :], in0=xin[cbuf][:, :],
            scalar1=negmu_t[:, g:g + 1], scalar2=rstd_t[:, g:g + 1],
            op0=ALU.add, op1=ALU.mult).then_inc(dve.h, 1))
        u_done = dve.bump()
        ci = g % CPS
        for half in range(2):
            tp = tpA if half == 0 else tpB
            PE.wait(dve, u_done)
            PE.wait(dve, tp_free[half])
            for k in range(4):
                dcc = 4 * half + k
                if k == 3:
                    PE(lambda e, tp=tp, k=k, dcc=dcc: e.transpose(
                        out=tp[:, 128 * k:128 * (k + 1)],
                        in_=u_t[:, 128 * dcc:128 * (dcc + 1)],
                        identity=id_bf[:, :]).then_inc(pe.h, 1))
                else:
                    PE(lambda e, tp=tp, k=k, dcc=dcc: e.transpose(
                        out=tp[:, 128 * k:128 * (k + 1)],
                        in_=u_t[:, 128 * dcc:128 * (dcc + 1)],
                        identity=id_bf[:, :]))
            t_done = pe.bump()
            DVE.wait(pe, t_done)
            DVE.wait(pe, uT_gate_pe)
            DVE(lambda e, tp=tp, half=half, ci=ci: e.tensor_copy(
                out=uT3[:, 4 * half:4 * (half + 1), 128 * ci:128 * (ci + 1)],
                in_=tp[:, 0:512]).then_inc(dve.h, 1))
            tp_free[half] = dve.bump()
        return u_done, tp_free[1]

    def emit_sums(src_tile, g):
        DVE(lambda e: e.tensor_tensor(out=xsq[:, :], in0=src_tile[:, :],
                                      in1=src_tile[:, :], op=ALU.mult))
        DVE(lambda e, g=g: e.reduce_sum(out=sums_t[:, 2 * g + 1:2 * g + 2],
                                        in_=xsq[:, :], axis=AX.X))
        DVE(lambda e, g=g: e.reduce_sum(out=sums_t[:, 2 * g:2 * g + 1],
                                        in_=src_tile[:, :],
                                        axis=AX.X).then_inc(dve.h, 1))
        return dve.bump()

    def emit_lntable(sums_done_dve):
        W = NCH
        sx_ap = sums_t[:, :].rearrange("p (g two) -> p two g", two=2)
        ACT.wait(dve, sums_done_dve)
        ACT(lambda e: e.activation(out=negmu_t[:, :], in_=sx_ap[:, 0:1, :].rearrange("p a g -> p (a g)"),
                                   func=AF.Identity, scale=-1.0 / D))
        ACT(lambda e: e.activation(out=lnrows[:, 0:W], in_=sx_ap[:, 0:1, :].rearrange("p a g -> p (a g)"),
                                   func=AF.Square, scale=1.0 / D))
        ACT(lambda e: e.activation(out=lnrows[:, W:2 * W], in_=sx_ap[:, 1:2, :].rearrange("p a g -> p (a g)"),
                                   func=AF.Identity, scale=1.0 / D).then_inc(act.h, 1))
        aff = act.bump()
        DVE.wait(act, aff)
        DVE(lambda e: e.tensor_tensor(out=lnrows[:, W:2 * W], in0=lnrows[:, W:2 * W],
                                      in1=lnrows[:, 0:W],
                                      op=ALU.subtract).then_inc(dve.h, 1))
        var_done = dve.bump()
        ACT.wait(dve, var_done)
        ACT(lambda e: e.activation(out=lnrows[:, 0:W], in_=lnrows[:, W:2 * W],
                                   func=AF.Ln, bias=EPS, scale=1.0).then_inc(act.h, 1))
        lnv = act.bump()
        DVE.wait(act, lnv)
        DVE(lambda e: e.tensor_copy(out=lnrows[:, W:2 * W],
                                    in_=lnrows[:, 0:W]).then_inc(dve.h, 1))
        cp = dve.bump()
        ACT.wait(dve, cp)
        ACT(lambda e: e.activation(out=rstd_t[:, :], in_=lnrows[:, W:2 * W],
                                   func=AF.Exp, scale=-0.5).then_inc(act.h, 1))
        return act.bump()

    for o, i in ((cst, cst_h), (id_bf, id_bf_h),
                 (ones_col, ones_col_h), (ones_row, ones_row_h),
                 (bqk_pt, bqk_pt_h), (lnw_pt, lnw_pt_h), (lnb_pt, lnb_pt_h)):
        dmain(SY, dmc, o[:, :], i)
    dmain(PO, dmg, bv_t[:, :], bv_t_h)
    consts_done = dmc.n

    DVE.wait(dmc, consts_done)
    DVE(lambda e: e.tensor_copy(out=lnb_bf[:, :], in_=lnb_pt[:, :]).then_inc(dve.h, 1))
    dve.bump()

    ffn_stack = ExitStack()
    wfc_sb = sb("wfc_sb", [128, 8 * HID], BF16, ffn_stack)
    wp_sb = sb("wp_sb", [128, 32 * D], BF16, ffn_stack)

    G3SPLIT = 8
    gT3_1a = sb("gT3a", [128, G3SPLIT, TSB], BF16, ffn_stack)
    prep_stack = ExitStack()
    id_f32 = sb("id_f32_s", [128, 128], F32, prep_stack)
    dmain(SY, dmc, id_f32[:, :], id_f32_h)
    biasrow = sb("biasrow", [1, D], BF16, prep_stack)
    thr = sb("thr", [128, 4], F32, prep_stack)
    absacc = sb("absacc", [128, 66], F32, prep_stack)
    gam = sb("gam", [1, 8], F32, prep_stack)
    wstage = [sb(f"wstage{i}", [128, 1024], F32, prep_stack) for i in range(3)]
    sgn_b = [sb(f"sgn{i}", [128, 1024], BF16, prep_stack) for i in range(2)]
    abs_b = [sb(f"absb{i}", [128, 1024], F32, prep_stack) for i in range(2)]
    msk_b = [sb(f"mskb{i}", [128, 1024], BF16, prep_stack) for i in range(2)]
    onesrow_bf = sb("onesrow_bf", [1, 128], BF16, prep_stack)

    def wtile_ap(which, i):
        if which == 0:
            dc, j = divmod(i, 4)
            return wfcT_h[128 * dc:128 * (dc + 1), 1024 * j:1024 * (j + 1)]
        return wpT_h[128 * i:128 * (i + 1), :]

    def wtile_tgt(which, i):
        if which == 0:
            dc, j = divmod(i, 4)
            return wfc_sb[:, dc * HID + 1024 * j: dc * HID + 1024 * (j + 1)]
        return wp_sb[:, 1024 * i: 1024 * (i + 1)]

    stage_rd = [0, 0, 0]
    prepass_rd = [0, 0]
    sums_last = 0
    for which in (0, 1):
        for i in range(32):
            b = (which * 32 + i) % 3
            SY.wait(dve, stage_rd[b])
            v = dmain(SY, sw[b], wstage[b][:, :], wtile_ap(which, i))
            DVE.wait(sw[b], v)
            DVE(lambda e, b=b, col=which * 32 + i + 2: e.reduce_sum(
                out=absacc[:, col:col + 1], in_=wstage[b][:, :], axis=AX.X,
                apply_absolute_value=True).then_inc(dve.h, 1))
            stage_rd[b] = dve.bump()
            if which == 0:
                cbuf = i % 2
                SY.wait(dve, prepass_rd[cbuf])
                vin = dmain(SY, sx[cbuf], xin[cbuf][:, :],
                            x_in[128 * i:128 * (i + 1), :])
                DVE.wait(sx[cbuf], vin)
                sums_last = emit_sums(xin[cbuf], i)
                prepass_rd[cbuf] = sums_last
    DVE(lambda e: e.memset(onesrow_bf[0:1, :], 1.0).then_inc(dve.h, 1))
    p1_done = dve.bump()
    ACT.wait(dve, p1_done)
    ACT(lambda e: e.nop().then_inc(act.h, 1))
    p1_echo = act.bump()
    DVE.wait(act, p1_echo)
    for w in (0, 1):
        DVE(lambda e, w=w: e.reduce_sum(
            out=absacc[:, w:w + 1], in_=absacc[:, 32 * w + 2:32 * w + 34], axis=AX.X))
    DVE(lambda e: e.tensor_copy(out=gam[0:1, 4:5], in_=gam[0:1, 4:5]).then_inc(dve.h, 1))
    abscols_done = dve.bump()
    PE.wait(dve, abscols_done)
    PE(lambda e: e.transpose(out=psC[0:1, 0:128], in_=absacc[:, 0:1],
                             identity=id_f32[:, :]))
    PE(lambda e: e.transpose(out=psC[0:1, 128:256], in_=absacc[:, 1:2],
                             identity=id_f32[:, :]).then_inc(pe.h, 1))
    tr_done = pe.bump()
    DVE.wait(pe, tr_done)
    for w in (0, 1):
        DVE(lambda e, w=w: e.reduce_sum(out=gam[0:1, w:w + 1],
                                        in_=psC[0:1, 128 * w:128 * (w + 1)], axis=AX.X))
    DVE(lambda e: e.tensor_copy(out=gam[0:1, 6:8], in_=gam[0:1, 0:2]).then_inc(dve.h, 1))
    sums_g = dve.bump()
    ACT.wait(dve, sums_g)
    ACT(lambda e: e.activation(out=gam[0:1, 2:4], in_=gam[0:1, 0:2], func=AF.Identity,
                               scale=1.0 / (2.0 * HID * D),
                               bias=0.5e-6).then_inc(act.h, 1))
    thr_row = act.bump()
    PE.wait(act, thr_row)
    PE(lambda e: e.matmul(out=psC[:, 4:6], lhsT=ones_row[0:1, :], rhs=gam[0:1, 2:4],
                          start=True, stop=True).then_inc(pe.h, 1))
    thr_rep = pe.bump()
    ACT.wait(pe, thr_rep)
    ACT(lambda e: e.activation(out=thr[:, 0:1], in_=psC[:, 4:5], func=AF.Identity))
    ACT(lambda e: e.activation(out=thr[:, 1:2], in_=psC[:, 4:5], func=AF.Identity,
                               scale=-1.0))
    ACT(lambda e: e.activation(out=thr[:, 2:3], in_=psC[:, 5:6], func=AF.Identity))
    ACT(lambda e: e.activation(out=thr[:, 3:4], in_=psC[:, 5:6], func=AF.Identity,
                               scale=-1.0).then_inc(act.h, 1))
    thr_done_act = act.bump()
    thr_done = None
    t1 = [emit_lntable(sums_last)]

    stage_rd2 = [(dve, stage_rd[0]), (dve, stage_rd[1]), (dve, stage_rd[2])]
    buf_rd = [0, 0]
    msk_rd = [0, 0]
    wfc_jgate = {}
    kctr = [0]
    for which in (1, 0):
        for idx in range(32):
            i = idx if which == 1 else (idx % 8) * 4 + idx // 8
            b = kctr[0] % 3
            kctr[0] += 1
            lb = idx % 2
            tgt = wtile_tgt(which, i)
            SY.wait(*stage_rd2[b])
            v = dmain(SY, sw[b], wstage[b][:, :], wtile_ap(which, i))
            ACT.wait(sw[b], v)
            ACT.wait(pool, buf_rd[lb])
            ACT(lambda e, b=b, lb=lb: e.activation(
                out=sgn_b[lb][:, :], in_=wstage[b][:, :], func=AF.Sign))
            ACT(lambda e, b=b, lb=lb: e.activation(
                out=abs_b[lb][:, :], in_=wstage[b][:, :],
                func=AF.Abs).then_inc(act.h, 1))
            sa_done = act.bump()
            stage_rd2[b] = (act, sa_done)
            DVE.wait(act, sa_done)
            DVE.wait(act, thr_done_act)
            DVE.wait(pool, msk_rd[lb])
            DVE(lambda e, lb=lb, w=which: e.tensor_scalar(
                out=msk_b[lb][:, :], in0=abs_b[lb][:, :],
                scalar1=thr[:, 2 * w:2 * w + 1], scalar2=None,
                op0=ALU.is_gt, op1=ALU.bypass).then_inc(dve.h, 1))
            mk_done = dve.bump()
            PO.wait(dve, mk_done)
            PO(lambda e, lb=lb, tgt=tgt: e.tensor_tensor(
                out=tgt, in0=sgn_b[lb][:, :], in1=msk_b[lb][:, :],
                op=ALU.mult).then_inc(pool.h, 1))
            wq_done = pool.bump()
            buf_rd[lb] = wq_done
            msk_rd[lb] = wq_done
            if which == 0 and idx % 8 == 7:
                wfc_jgate[8 * (idx // 8)] = wq_done
            if which == 1:
                PE.wait(pool, wq_done)
                PE(lambda e, i=i, tgt=tgt: e.matmul(
                    out=psA[0:1, 0:512], lhsT=lnb_bf[:, i:i + 1], rhs=tgt[:, 0:512],
                    start=(i == 0), stop=(i == 31)))
                PE(lambda e, i=i, tgt=tgt: e.matmul(
                    out=psA[0:1, 512:1024], lhsT=lnb_bf[:, i:i + 1],
                    rhs=tgt[:, 512:1024], start=(i == 0),
                    stop=(i == 31)).then_inc(pe.h, 1))
                br_mm = pe.bump()
                DVE.wait(pe, br_mm)
                DVE(lambda e, tgt=tgt, i=i: e.tensor_scalar(
                    out=tgt, in0=tgt, scalar1=lnw_pt[:, i:i + 1], scalar2=None,
                    op0=ALU.mult, op1=ALU.bypass).then_inc(dve.h, 1))
                sc_done = dve.bump()
                PE.wait(dve, sc_done)
                PE(lambda e, i=i, tgt=tgt: e.matmul(
                    out=psB[0:1, 0:512], lhsT=ones_col[:, 0:1], rhs=tgt[:, 0:512],
                    start=(i == 0), stop=(i == 31)))
                PE(lambda e, i=i, tgt=tgt: e.matmul(
                    out=psB[0:1, 512:1024], lhsT=ones_col[:, 0:1],
                    rhs=tgt[:, 512:1024], start=(i == 0),
                    stop=(i == 31)).then_inc(pe.h, 1))
                pe.bump()
    wsum_mm_done = pe.n
    wp_pool_last = pool.n
    DVE.wait(pe, wsum_mm_done)
    DVE(lambda e: e.tensor_scalar(out=biasrow[0:1, :], in0=psA[0:1, :], scalar1=0.5,
                                  scalar2=None, op0=ALU.mult, op1=ALU.bypass))
    DVE(lambda e: e.tensor_copy(out=wsum_row[0:1, :], in_=psB[0:1, :]).then_inc(dve.h, 1))
    rows_done = dve.bump()
    PE.wait(dve, rows_done)
    PE(lambda e: e.matmul(out=psA[:, 0:512], lhsT=onesrow_bf[0:1, :],
                          rhs=biasrow[0:1, 0:512], start=True, stop=True))
    PE(lambda e: e.matmul(out=psA[:, 512:1024], lhsT=onesrow_bf[0:1, :],
                          rhs=biasrow[0:1, 512:1024], start=True,
                          stop=True).then_inc(pe.h, 1))
    fb_mm = pe.bump()
    DVE.wait(pe, fb_mm)
    DVE(lambda e: e.tensor_copy(out=fbias_t[:, :], in_=psA[:, :]).then_inc(dve.h, 1))
    prep_dve_last = dve.bump()

    prep_stack.close()
    gT3_1 = (gT3_1a, sb("gT3b", [128, 32 - G3SPLIT, TSB], BF16, ffn_stack))

    def emit_ffn(src_hbm, dst_hbm, wfc_t, wp_t, gT3, table_done, next_sums, spill=False, xin_seed=(0, 0), w_gates=None, wp_gate=None, spill_gate=None, g3b_gate=None):
        NSBr = DEV_NSB if DEV_NSB else NSB
        ffn_local = ExitStack()
        tag = "a" if spill else "b"
        xres = [sb(f"xres{tag}{i}", [128, D], F32, ffn_local) for i in range(2)]

        def g3(h0, h1, c0=0, c1=TSB):
            if h1 <= G3SPLIT:
                return gT3[0][:, h0:h1, c0:c1]
            assert h0 >= G3SPLIT, (h0, h1)
            return gT3[1][:, h0 - G3SPLIT:h1 - G3SPLIT, c0:c1]
        spill_done = 0
        u_done_ms = {-2: xin_seed[0], -1: xin_seed[1]}
        uT_ready_sb = {}
        stt_done = {}
        psum_free = {}
        out_done = {}
        sqmm_last = [0]
        mm2_last = 0
        mm1_last = 0
        stats_pe_last = [0]

        def ln_front(g):
            cbuf = g % 2
            SY.wait(dve, u_done_ms.get(g - 2, 0))
            vin = dmain(SY, sx[cbuf], xin[cbuf][:, :],
                        src_hbm[128 * g:128 * (g + 1), :])
            DVE.wait(sx[cbuf], vin)
            DVE.wait(act, table_done[0])
            DVE(lambda e, g=g: e.tensor_scalar(
                out=u_t[:, :], in0=xin[cbuf][:, :],
                scalar1=negmu_t[:, g:g + 1], scalar2=rstd_t[:, g:g + 1],
                op0=ALU.add, op1=ALU.mult).then_inc(dve.h, 1))
            u_done_ms[g] = dve.bump()

        def ln_transpose(g):
            ci = g % CPS
            for half in range(2):
                tp = tpA if half == 0 else tpB
                PE.wait(dve, u_done_ms[g])
                PE.wait(dve, tp_free[half])
                for k in range(4):
                    dcc = 4 * half + k
                    if k == 3:
                        PE(lambda e, tp=tp, k=k, dcc=dcc: e.transpose(
                            out=tp[:, 128 * k:128 * (k + 1)],
                            in_=u_t[:, 128 * dcc:128 * (dcc + 1)],
                            identity=id_bf[:, :]).then_inc(pe.h, 1))
                    else:
                        PE(lambda e, tp=tp, k=k, dcc=dcc: e.transpose(
                            out=tp[:, 128 * k:128 * (k + 1)],
                            in_=u_t[:, 128 * dcc:128 * (dcc + 1)],
                            identity=id_bf[:, :]))
                t_done = pe.bump()
                DVE.wait(pe, t_done)
                DVE.wait(pe, mm1_last)
                DVE(lambda e, tp=tp, half=half, ci=ci: e.tensor_copy(
                    out=uT3[:, 4 * half:4 * (half + 1), 128 * ci:128 * (ci + 1)],
                    in_=tp[:, 0:512]).then_inc(dve.h, 1))
                tp_free[half] = dve.bump()
            uT_ready_sb[g // CPS] = tp_free[1]

        for ci in range(CPS):
            ln_front(ci)
            ln_transpose(ci)
        if spill:
            if spill_gate:
                SY.wait(*spill_gate)
            dmain(SY, dmw, wspill[:, 0:32768], wfc_t[:, :])
            spill_done = dmain(SY, dmw, wspill[:, 32768:65536], wp_t[:, :])

        for sbi in range(NSBr):
            hp_free = [0, 0, 0, 0]
            PE.wait(dve, uT_ready_sb[sbi])
            PE.wait(dve, psum_free.get(sbi * CPS - 2, 0))
            gelu_last = 0
            gelu_psA_last = 0
            for hf in range(32):
                pr = hf % 4
                tgt = psA if pr < 2 else psB
                lo = 512 * (pr % 2)
                if hf == 2:
                    PE.wait(dve, psum_free.get(sbi * CPS - 1, 0))
                if sbi == 0 and w_gates and hf in w_gates:
                    gv = w_gates[hf]
                    for sv in (gv if isinstance(gv, list) else [gv]):
                        PE.wait(*sv)
                PE.wait(act, hp_free[pr])
                for dc in range(8):
                    if dc == 7:
                        PE(lambda e, hf=hf, dc=dc, tgt=tgt, lo=lo: e.matmul(
                            out=tgt[:, lo:lo + 512],
                            lhsT=wfc_t[:, dc * HID + 128 * hf:dc * HID + 128 * (hf + 1)],
                            rhs=r2(uT3[:, dc:dc + 1, :]),
                            start=False, stop=True).then_inc(pe.h, 1))
                    else:
                        PE(lambda e, hf=hf, dc=dc, tgt=tgt, lo=lo: e.matmul(
                            out=tgt[:, lo:lo + 512],
                            lhsT=wfc_t[:, dc * HID + 128 * hf:dc * HID + 128 * (hf + 1)],
                            rhs=r2(uT3[:, dc:dc + 1, :]),
                            start=(dc == 0), stop=False))
                mm1_done = pe.bump()
                if hf % 2 == 1:
                    ACT.wait(pe, mm1_done)
                    if sbi == 0 and hf == G3SPLIT + 1 and g3b_gate:
                        ACT.wait(*g3b_gate)
                    if sbi > 0 and hf == 1:
                        ACT.wait(pe, mm2_last)
                    ACT(lambda e, hf=hf, tgt=tgt: e.activation(
                        out=r2(g3(hf - 1, hf + 1)), in_=tgt[:, :],
                        func=AF.Gelu).then_inc(act.h, 1))
                    gv = act.bump()
                    gelu_last = gv
                    if pr < 2:
                        gelu_psA_last = gv
                    hp_free[pr] = gv
                    hp_free[pr - 1] = gv
                    PE.wait(act, gv)
                    for k in range(2):
                        PE(lambda e, hf=hf, k=k: e.matmul(
                            out=psC[0:1, 0:512], lhsT=ones_col[:, 0:1],
                            rhs=r2(g3(hf - 1 + k, hf + k)),
                            start=(hf == 1 and k == 0), stop=(hf == 31 and k == 1)))
                    for k in range(2):
                        DVE.wait(act, gv)
                        DVE.wait(pe, sqmm_last[0])
                        DVE(lambda e, hf=hf, k=k: e.tensor_tensor(
                            out=u_t[:, 0:512],
                            in0=r2(g3(hf - 1 + k, hf + k)),
                            in1=r2(g3(hf - 1 + k, hf + k)),
                            op=ALU.mult).then_inc(dve.h, 1))
                        gq = dve.bump()
                        PE.wait(dve, gq)
                        PE(lambda e, hf=hf, k=k: e.matmul(
                            out=psC[0:1, 512:1024], lhsT=ones_col[:, 0:1],
                            rhs=u_t[:, 0:512],
                            start=(hf == 1 and k == 0),
                            stop=(hf == 31 and k == 1)).then_inc(pe.h, 1))
                        sqmm_last[0] = pe.bump()
            stats_pe_last[0] = pe.n
            mm1_last = pe.n
            DVE.wait(pe, stats_pe_last[0])
            DVE(lambda e: e.tensor_scalar(out=rows[0:1, 0:TSB], in0=psC[0:1, 0:512],
                                          scalar1=1.0 / HID, scalar2=None,
                                          op0=ALU.mult, op1=ALU.bypass))
            if mm2_last:
                DVE.wait(pe, mm2_last)
            DVE(lambda e: e.tensor_scalar(out=negmurow[0:1, :], in0=psC[0:1, 0:512],
                                          scalar1=-1.0 / HID, scalar2=None,
                                          op0=ALU.mult, op1=ALU.bypass))
            DVE(lambda e: e.tensor_scalar(out=rows[0:1, TSB:2 * TSB],
                                          in0=psC[0:1, 512:1024],
                                          scalar1=1.0 / HID, scalar2=None,
                                          op0=ALU.mult,
                                          op1=ALU.bypass).then_inc(dve.h, 1))
            rows_a = dve.bump()
            ACT.wait(dve, rows_a)
            ACT(lambda e: e.activation(out=rows[0:1, 2 * TSB:3 * TSB],
                                       in_=rows[0:1, 0:TSB],
                                       func=AF.Square).then_inc(act.h, 1))
            musq_b = act.bump()
            DVE.wait(act, musq_b)
            DVE(lambda e: e.tensor_tensor(out=rows[0:1, TSB:2 * TSB],
                                          in0=rows[0:1, TSB:2 * TSB],
                                          in1=rows[0:1, 2 * TSB:3 * TSB],
                                          op=ALU.subtract).then_inc(dve.h, 1))
            var_row = dve.bump()
            ACT.wait(dve, var_row)
            ACT(lambda e: e.activation(out=rows[0:1, 0:TSB],
                                       in_=rows[0:1, TSB:2 * TSB], func=AF.Ln,
                                       bias=EPS, scale=1.0).then_inc(act.h, 1))
            lnv_row = act.bump()
            DVE.wait(act, lnv_row)
            DVE(lambda e: e.tensor_copy(out=rows[0:1, 2 * TSB:3 * TSB],
                                        in_=rows[0:1, 0:TSB]).then_inc(dve.h, 1))
            cp_row = dve.bump()
            ACT.wait(dve, cp_row)
            ACT(lambda e: e.activation(out=rows[0:1, TSB:2 * TSB],
                                       in_=rows[0:1, 2 * TSB:3 * TSB], func=AF.Exp,
                                       scale=-0.5,
                                       bias=float(np.log(0.5))).then_inc(act.h, 1))
            rstd_row = act.bump()
            if DEV_RSTD_AFTER == "before":
                PE.wait(act, rstd_row)
                for cc in range(CPS):
                    if cc == CPS - 1:
                        PE(lambda e, cc=cc: e.matmul(
                            out=psC[:, 8 + cc:9 + cc],
                            lhsT=rows[0:1, TSB + 128 * cc:TSB + 128 * (cc + 1)],
                            rhs=ones_row[0:1, 0:1], start=True,
                            stop=True).then_inc(pe.h, 1))
                    else:
                        PE(lambda e, cc=cc: e.matmul(
                            out=psC[:, 8 + cc:9 + cc],
                            lhsT=rows[0:1, TSB + 128 * cc:TSB + 128 * (cc + 1)],
                            rhs=ones_row[0:1, 0:1], start=True, stop=True))
                rstdT = pe.bump()
                ACT.wait(pe, rstdT)
                if stt_done:
                    ACT.wait(dve, stt_done[sbi * CPS - 1])
                ACT(lambda e: e.activation(out=rstd05[:, :], in_=psC[:, 8:8 + CPS],
                                           func=AF.Identity).then_inc(act.h, 1))
                rstd05_done = act.bump()
            mm2_done = {}
            for ci in range(CPS):
                g = sbi * CPS + ci
                fps = psA if ci % 2 == 0 else psB
                PE.wait(dve, psum_free.get(g - 2, 0))
                if ci == 0:
                    PE.wait(act, gelu_psA_last)
                    if sbi == 0 and wp_gate:
                        PE.wait(*wp_gate)
                if ci == 1:
                    PE.wait(act, gelu_last)
                for h2 in range(32):
                    for half in range(2):
                        PE(lambda e, h2=h2, half=half, ci=ci, fps=fps: e.matmul(
                            out=fps[:, 512 * half:512 * (half + 1)],
                            lhsT=r2(g3(h2, h2 + 1, 128 * ci, 128 * (ci + 1))),
                            rhs=wp_t[:, 1024 * h2 + 512 * half:1024 * h2 + 512 * (half + 1)],
                            start=(h2 == 0), stop=False))
                if ci == 0:
                    PE.wait(dve, rows_a)
                for half in range(2):
                    if half == 1:
                        PE(lambda e, half=half, ci=ci, fps=fps: e.matmul(
                            out=fps[:, 512 * half:512 * (half + 1)],
                            lhsT=negmurow[0:1, 128 * ci:128 * (ci + 1)],
                            rhs=wsum_row[0:1, 512 * half:512 * (half + 1)],
                            start=False, stop=True).then_inc(pe.h, 1))
                    else:
                        PE(lambda e, half=half, ci=ci, fps=fps: e.matmul(
                            out=fps[:, 512 * half:512 * (half + 1)],
                            lhsT=negmurow[0:1, 128 * ci:128 * (ci + 1)],
                            rhs=wsum_row[0:1, 512 * half:512 * (half + 1)],
                            start=False, stop=True))
                mm2_done[ci] = pe.bump()
                if ci == CPS - 1:
                    mm2_last = mm2_done[ci]
                if DEV_RSTD_AFTER == "ci1" and ci == 0:
                    PE.wait(act, rstd_row)
                    for cc in range(CPS):
                        if cc == CPS - 1:
                            PE(lambda e, cc=cc: e.matmul(
                                out=psC[:, 8 + cc:9 + cc],
                                lhsT=rows[0:1, TSB + 128 * cc:TSB + 128 * (cc + 1)],
                                rhs=ones_row[0:1, 0:1], start=True,
                                stop=True).then_inc(pe.h, 1))
                        else:
                            PE(lambda e, cc=cc: e.matmul(
                                out=psC[:, 8 + cc:9 + cc],
                                lhsT=rows[0:1, TSB + 128 * cc:TSB + 128 * (cc + 1)],
                                rhs=ones_row[0:1, 0:1], start=True, stop=True))
                    rstdT = pe.bump()
                    ACT.wait(pe, rstdT)
                    if stt_done:
                        ACT.wait(dve, stt_done[sbi * CPS - 1])
                    ACT(lambda e: e.activation(out=rstd05[:, :], in_=psC[:, 8:8 + CPS],
                                               func=AF.Identity).then_inc(act.h, 1))
                    rstd05_done = act.bump()
                if (ci == 1 and not DEV_RSTD_AFTER) or (DEV_RSTD_AFTER and ci == CPS - 1 and False):
                    PE.wait(act, rstd_row)
                    for cc in range(CPS):
                        if cc == CPS - 1:
                            PE(lambda e, cc=cc: e.transpose(
                                out=psC[:, 8 + cc:9 + cc],
                                in_=rows[0:1, TSB + 128 * cc:TSB + 128 * (cc + 1)],
                                identity=id_f32[0:1, 0:1]).then_inc(pe.h, 1))
                        else:
                            PE(lambda e, cc=cc: e.transpose(
                                out=psC[:, 8 + cc:9 + cc],
                                in_=rows[0:1, TSB + 128 * cc:TSB + 128 * (cc + 1)],
                                identity=id_f32[0:1, 0:1]))
                    rstdT = pe.bump()
                    ACT.wait(pe, rstdT)
                    if stt_done:
                        ACT.wait(dve, stt_done[sbi * CPS - 1])
                    ACT(lambda e: e.activation(out=rstd05[:, :], in_=psC[:, 8:8 + CPS],
                                               func=AF.Identity).then_inc(act.h, 1))
                    rstd05_done = act.bump()
                if sbi + 1 < NSBr:
                    ln_front(g + CPS)
                    ln_transpose(g + CPS)
                SY.wait(dve, psum_free.get(g - 2, 0))
                vres = dmain(SY, sxr[ci % 2], xres[ci % 2][:, :],
                             src_hbm[128 * g:128 * (g + 1), :])
                DVE.wait(pe, mm2_done[ci])
                DVE.wait(sxr[ci % 2], vres)
                DVE.wait(dmo, out_done.get(g - 1, 0))
                if ci == 0:
                    DVE.wait(act, rstd05_done)
                DVE(lambda e, ci=ci, fps=fps: e.scalar_tensor_tensor(
                    out=xout[:, :], in0=fps[:, :], scalar=rstd05[:, ci:ci + 1],
                    in1=xres[ci % 2][:, :], op0=ALU.mult,
                    op1=ALU.add).then_inc(dve.h, 1))
                psum_free[g] = dve.bump()
                DVE(lambda e: e.tensor_tensor(
                    out=xout[:, :], in0=xout[:, :], in1=fbias_t[:, :],
                    op=ALU.add).then_inc(dve.h, 1))
                stt_done[g] = dve.bump()
                SY.wait(dve, stt_done[g])
                out_done[g] = dmain(SY, dmo, dst_hbm[128 * g:128 * (g + 1), :],
                                    xout[:, :])
                if next_sums:
                    stt_done[g] = emit_sums(xout, g)
            if DEV_RSTD_AFTER is True:
                PE.wait(act, rstd_row)
                for cc in range(CPS):
                    if cc == CPS - 1:
                        PE(lambda e, cc=cc: e.matmul(
                            out=psC[:, 8 + cc:9 + cc],
                            lhsT=rows[0:1, TSB + 128 * cc:TSB + 128 * (cc + 1)],
                            rhs=ones_row[0:1, 0:1], start=True,
                            stop=True).then_inc(pe.h, 1))
                    else:
                        PE(lambda e, cc=cc: e.matmul(
                            out=psC[:, 8 + cc:9 + cc],
                            lhsT=rows[0:1, TSB + 128 * cc:TSB + 128 * (cc + 1)],
                            rhs=ones_row[0:1, 0:1], start=True, stop=True))
                rstdT = pe.bump()
                ACT.wait(pe, rstdT)
                if stt_done:
                    ACT.wait(dve, stt_done[sbi * CPS - 1])
                ACT(lambda e: e.activation(out=rstd05[:, :], in_=psC[:, 8:8 + CPS],
                                           func=AF.Identity).then_inc(act.h, 1))
                rstd05_done = act.bump()
        if spill:
            SY.wait(dmw, spill_done)
        ffn_local.close()

    def emit_attn(at_dst, table_done, next_sums=True):
        at_stack = ExitStack()
        qkT3 = sb("qkT3", [128, 16, SEQ], BF16, at_stack)
        vbuf = sb("vbuf", [128, 16, 16, 65], BF16, at_stack)
        o_st = sb("o_st", [128, 4, D], BF16, at_stack)
        mask_ut = sb("mask_ut_s", [128, 128], BF16, at_stack)
        mv = dmain(SY, dmc, mask_ut[:, :], mask_ut_h)
        DVE(lambda e: e.memset(vbuf[:, :, :, 64:65], 1.0).then_inc(dve.h, 1))
        dve.bump()

        qk_read_pe = 0
        v_read_pe = 0
        oev_prev = 0
        mm_prev_sb = 0
        stats_prev_sb = 0
        xin_carry = [0, 0]
        for s in range(NSEQ):
            qw_stack = ExitStack()
            wqk_sb = sb(f"wqk_sb{s}", [128, 8 * 2048], BF16, qw_stack)
            wv_sb = sb(f"wv_sb{s}", [128, 8 * 1024], BF16, qw_stack)
            PO.wait(pe, v_read_pe)
            wq_v = dmain(PO, dmg, wqk_sb[:, :].rearrange("p (c n) -> p c n", n=2048),
                         wqk_h.rearrange("(c p) n -> p c n", p=128))
            wq_v = dmain(PO, dmg, wv_sb[:, :].rearrange("p (c n) -> p c n", n=1024),
                         wv_h.rearrange("(c p) n -> p c n", p=128))
            xin_free = {-2: xin_carry[0], -1: xin_carry[1]}
            qk_done = 0
            v_evac = 0
            first_pe_wait = True
            for sbq in range(4):
                uT_ready = 0
                for ci in range(CPS):
                    gl = 4 * sbq + ci
                    g = 16 * s + gl
                    u_done, uT_ready = emit_ln_chunk(
                        xs1, g, gl % 2, xin_free.get(gl - 2, 0),
                        stats_prev_sb, mm_prev_sb, table_done)
                    xin_free[gl] = u_done
                qp_free = [0, 0, 0, 0]
                PE.wait(dve, uT_ready)
                if first_pe_wait:
                    PE.wait(dmg, wq_v)
                    if s == 1:
                        PE.wait(dve, oev_prev)
                    first_pe_wait = False
                for fc in range(16):
                    pr = fc % 4
                    tgt = psA if pr < 2 else psB
                    lo = 512 * (pr % 2)
                    PE.wait(act, qp_free[pr])
                    for dc in range(8):
                        if dc == 7:
                            PE(lambda e, fc=fc, dc=dc, tgt=tgt, lo=lo: e.matmul(
                                out=tgt[:, lo:lo + 512],
                                lhsT=wqk_sb[:, dc * 2048 + 128 * fc:dc * 2048 + 128 * (fc + 1)],
                                rhs=r2(uT3[:, dc:dc + 1, :]),
                                start=False, stop=True).then_inc(pe.h, 1))
                        else:
                            PE(lambda e, fc=fc, dc=dc, tgt=tgt, lo=lo: e.matmul(
                                out=tgt[:, lo:lo + 512],
                                lhsT=wqk_sb[:, dc * 2048 + 128 * fc:dc * 2048 + 128 * (fc + 1)],
                                rhs=r2(uT3[:, dc:dc + 1, :]),
                                start=(dc == 0), stop=False))
                    qmm = pe.bump()
                    ACT.wait(pe, qmm)
                    if s == 1 and sbq == 0 and fc == 0:
                        ACT.wait(pe, qk_read_pe)
                    ACT(lambda e, fc=fc, sbq=sbq, tgt=tgt, lo=lo: e.activation(
                        out=r2(qkT3[:, fc:fc + 1, 512 * sbq:512 * (sbq + 1)]),
                        in_=tgt[:, lo:lo + 512], func=AF.Identity,
                        bias=bqk_pt[:, fc:fc + 1], scale=1.0).then_inc(act.h, 1))
                    qp_free[pr] = act.bump()
                qk_done = qp_free[(15) % 4]
                for ci in range(CPS):
                    tc = 4 * sbq + ci
                    PE.wait(dve, v_evac)
                    if s == 1 and tc == 0:
                        PE.wait(pe, v_read_pe)
                    for dc in range(8):
                        for half in range(2):
                            if dc == 7 and half == 1:
                                PE(lambda e, dc=dc, half=half, ci=ci: e.matmul(
                                    out=psC[:, 512 * half:512 * (half + 1)],
                                    lhsT=r2(uT3[:, dc:dc + 1, 128 * ci:128 * (ci + 1)]),
                                    rhs=wv_sb[:, dc * 1024 + 512 * half:dc * 1024 + 512 * (half + 1)],
                                    start=(dc == 0), stop=True).then_inc(pe.h, 1))
                            else:
                                PE(lambda e, dc=dc, half=half, ci=ci: e.matmul(
                                    out=psC[:, 512 * half:512 * (half + 1)],
                                    lhsT=r2(uT3[:, dc:dc + 1, 128 * ci:128 * (ci + 1)]),
                                    rhs=wv_sb[:, dc * 1024 + 512 * half:dc * 1024 + 512 * (half + 1)],
                                    start=(dc == 0), stop=(dc == 7)))
                    vmm = pe.bump()
                    DVE.wait(pe, vmm)
                    if s == 1 and tc == 0:
                        DVE.wait(pe, v_read_pe)
                    DVE(lambda e, tc=tc: e.tensor_copy(
                        out=vbuf[:, tc:tc + 1, :, 0:64].rearrange("p a h i -> p (a h) i"),
                        in_=psC[:, :].rearrange("p (h i) -> p h i", i=64)).then_inc(dve.h, 1))
                    v_evac = dve.bump()
                mm_prev_sb = pe.n
            qkv_xin_last = max(xin_free.get(14, 0), xin_free.get(15, 0))
            qw_stack.close()
            av_stack = ExitStack()
            aT2 = [[sb(f"aT{s}_{par}_{h}", [128, 16, 512], BF16, av_stack)
                    for h in range(2)] for par in range(2)]
            res_done = {}
            out_prev = 0
            sp_gate = [0, 0]
            av_done_pe = {}
            oev_by_pair = {}
            pend = None

            def emit_av_block(pd):
                nonlocal oev_prev
                st_, p_, expd, maskd, par = pd
                c_ = st_ * NPAIR + p_
                PE.wait(dve, v_evac)
                PE.wait(dve, oev_by_pair.get(c_ - 1, oev_prev))
                avz = 0
                for qt in range(4):
                    qt_g = 4 * st_ + qt
                    for h in range(2):
                        hg = 2 * p_ + h
                        PE.wait(act, expd[(h, qt_g // 2)])
                        PE.wait(dve, maskd[(h, qt)])
                        for kt2 in range(qt_g + 1):
                            if kt2 == qt_g:
                                PE(lambda e, h=h, qt=qt, kt2=kt2, hg=hg, par=par: e.matmul(
                                    out=psC[:, 512 * h + 65 * qt:512 * h + 65 * qt + 65],
                                    lhsT=r2(aT2[par][h][:, kt2:kt2 + 1, 128 * qt:128 * (qt + 1)]),
                                    rhs=vbuf[:, kt2:kt2 + 1, hg:hg + 1, :].rearrange("p a h i -> p (a h i)"),
                                    start=(kt2 == 0), stop=True).then_inc(pe.h, 1))
                            else:
                                PE(lambda e, h=h, qt=qt, kt2=kt2, hg=hg, par=par: e.matmul(
                                    out=psC[:, 512 * h + 65 * qt:512 * h + 65 * qt + 65],
                                    lhsT=r2(aT2[par][h][:, kt2:kt2 + 1, 128 * qt:128 * (qt + 1)]),
                                    rhs=vbuf[:, kt2:kt2 + 1, hg:hg + 1, :].rearrange("p a h i -> p (a h i)"),
                                    start=(kt2 == 0), stop=False))
                        avz = pe.bump()
                DVE.wait(pe, avz)
                for h in range(2):
                    for qt in range(4):
                        DVE(lambda e, h=h, qt=qt: e.reciprocal(
                            out=stat[0][:, 4 * h + qt:4 * h + qt + 1],
                            in_=psC[:, 512 * h + 65 * qt + 64:512 * h + 65 * qt + 65]))
                DVE(lambda e: e.nop().then_inc(dve.h, 1))
                rec_done = dve.bump()
                ACT.wait(dve, rec_done)
                ACT(lambda e: e.nop().then_inc(act.h, 1))
                rec_echo = act.bump()
                DVE.wait(act, rec_echo)
                for h in range(2):
                    hg = 2 * p_ + h
                    for qt in range(4):
                        DVE(lambda e, h=h, qt=qt, hg=hg: e.tensor_scalar(
                            out=r2(o_st[:, qt:qt + 1, 64 * hg:64 * (hg + 1)]),
                            in0=psC[:, 512 * h + 65 * qt:512 * h + 65 * qt + 64],
                            scalar1=stat[0][:, 4 * h + qt:4 * h + qt + 1],
                            scalar2=None, op0=ALU.mult,
                            op1=ALU.bypass).then_inc(dve.h, 1))
                        oev_prev = dve.bump()
                av_done_pe[c_] = pe.n
                oev_by_pair[c_] = oev_prev

            def emit_residual(st_):
                nonlocal out_prev
                DVE(lambda e: e.nop().then_inc(dve.h, 1))
                ob = dve.bump()
                ACT.wait(dve, ob)
                ACT(lambda e: e.nop().then_inc(act.h, 1))
                ob_echo = act.bump()
                DVE.wait(act, ob_echo)
                for qt in range(4):
                    gl = 4 * st_ + qt
                    g = 16 * s + gl
                    cbuf = gl % 2
                    gate = max(res_done.get(gl - 2, 0), qkv_xin_last)
                    SY.wait(dve, gate)
                    vin = dmain(SY, sx[cbuf], xin[cbuf][:, :],
                                xs1[128 * g:128 * (g + 1), :])
                    PO.wait(sx[cbuf], vin)
                    PO.wait(dmg, 16)
                    PO(lambda e, cbuf=cbuf: e.tensor_tensor(
                        out=xin[cbuf][:, :], in0=xin[cbuf][:, :], in1=bv_t[:, :],
                        op=ALU.add).then_inc(pool.h, 1))
                    pb_ = pool.bump()
                    DVE.wait(pool, pb_)
                    DVE.wait(dmo, out_prev)
                    DVE(lambda e, qt=qt, cbuf=cbuf: e.scalar_tensor_tensor(
                        out=xout[:, :], in0=r2(o_st[:, qt:qt + 1, :]), scalar=1.0,
                        in1=xin[cbuf][:, :], op0=ALU.mult,
                        op1=ALU.add).then_inc(dve.h, 1))
                    res_done[gl] = dve.bump()
                    SY.wait(dve, res_done[gl])
                    out_prev = dmain(SY, dmo, at_dst[128 * g:128 * (g + 1), :],
                                     xout[:, :])
                    if next_sums:
                        res_done[gl] = emit_sums(xout, g)

            for st in range(4):
                qlo = 512 * st
                for p in range(NPAIR):
                    c = st * NPAIR + p
                    par = c % 2
                    expd = {}
                    maskd = {}
                    nkt = 4 * st + 4
                    for k0 in range(0, nkt, 2):
                        for h in range(2):
                            S = psA if h == 0 else psB
                            pb = 64 * h
                            PE.wait(act, sp_gate[h])
                            if k0 == 0 and h == 0:
                                PE.wait(act, qk_done)
                                PE.wait(dmc, mv)
                            for ki in range(2):
                                kt = k0 + ki
                                qstart = max(qlo, 128 * kt)
                                nq = qlo + 512 - qstart
                                lo = 512 * ki + (qstart - qlo)
                                if ki == 1:
                                    PE(lambda e, S=S, lo=lo, pb=pb, p=p, kt=kt,
                                       qstart=qstart, qlo=qlo, nq=nq: e.matmul(
                                        out=S[:, lo:lo + nq],
                                        lhsT=r2(qkT3[pb:pb + 64, 8 + p:9 + p, 128 * kt:128 * (kt + 1)]),
                                        rhs=r2(qkT3[pb:pb + 64, p:p + 1, qstart:qlo + 512]),
                                        start=True, stop=True).then_inc(pe.h, 1))
                                else:
                                    PE(lambda e, S=S, lo=lo, pb=pb, p=p, kt=kt,
                                       qstart=qstart, qlo=qlo, nq=nq: e.matmul(
                                        out=S[:, lo:lo + nq],
                                        lhsT=r2(qkT3[pb:pb + 64, 8 + p:9 + p, 128 * kt:128 * (kt + 1)]),
                                        rhs=r2(qkT3[pb:pb + 64, p:p + 1, qstart:qlo + 512]),
                                        start=True, stop=True))
                            smm = pe.bump()
                            ACT.wait(pe, smm)
                            ACT.wait(pe, av_done_pe.get(c - 2, v_read_pe))
                            ACT(lambda e, h=h, k0=k0, S=S, par=par: e.activation(
                                out=r2(aT2[par][h][:, k0:k0 + 2, :]),
                                in_=S[:, 0:1024], func=AF.Exp, scale=0.125,
                                bias=EXP_SHIFT).then_inc(act.h, 1))
                            ev = act.bump()
                            sp_gate[h] = ev
                            expd[(h, k0 // 2)] = ev
                            if k0 >= 4 * st:
                                for ki in range(2):
                                    kt = k0 + ki
                                    qt = kt - 4 * st
                                    DVE.wait(act, ev)
                                    DVE(lambda e, h=h, kt=kt, qt=qt, par=par: e.tensor_tensor(
                                        out=r2(aT2[par][h][:, kt:kt + 1, 128 * qt:128 * (qt + 1)]),
                                        in0=r2(aT2[par][h][:, kt:kt + 1, 128 * qt:128 * (qt + 1)]),
                                        in1=mask_ut[:, :],
                                        op=ALU.mult).then_inc(dve.h, 1))
                                    maskd[(h, qt)] = dve.bump()
                    if pend is not None:
                        emit_av_block(pend)
                        if pend[1] == NPAIR - 1:
                            emit_residual(pend[0])
                    pend = (st, p, expd, maskd, par)
            emit_av_block(pend)
            emit_residual(3)
            pend = None
            qk_read_pe = pe.n
            v_read_pe = pe.n
            av_stack.close()
            xin_carry = [res_done.get(14, 0), res_done.get(15, 0)]
        at_stack.close()

    f1_dst = xs1 if len(RUN_PHASES) > 1 else y_out
    at_dst = xs2 if "f2" in RUN_PHASES else y_out
    if "f1" in RUN_PHASES:
        emit_ffn(x_in, f1_dst, wfc_sb, wp_sb, gT3_1, t1,
                 next_sums=("attn" in RUN_PHASES),
                 spill=("attn" in RUN_PHASES or "f2" in RUN_PHASES),
                 xin_seed=(prepass_rd[0], prepass_rd[1]),
                 w_gates={hf: ([(pool, ms)] + ([(act, thr_done_act)] if hf == 0 else []))
                          for hf, ms in wfc_jgate.items()},
                 wp_gate=(dve, prep_dve_last),
                 spill_gate=(pool, wp_pool_last),
                 g3b_gate=(pool, wp_pool_last))
        if len(RUN_PHASES) == 1:
            if DEBUG_TAPS:
                dmain(SY, dmc, dbg_gT2[:, 0:G3SPLIT * TSB], r2(gT3_1[0][:, :, :]))
                dmain(SY, dmc, dbg_wf, wfc_sb[:, :])
                dmain(SY, dmc, dbg_wp, wp_sb[:, :])
            SY.wait(dmo, dmo.n)
            SY.wait(sx[0], sx[0].n)
            SY.wait(sx[1], sx[1].n)
            SY.wait(sw[0], sw[0].n)
            SY.wait(sw[1], sw[1].n)
            SY.wait(sw[2], sw[2].n)
            SY.wait(dmw, dmw.n)
            SY.wait(dmc, dmc.n)
            SY.wait(dmg, dmg.n)
        flush_block()

    ffn_stack.close()

    if "attn" in RUN_PHASES:
        SY.wait(dmo, dmo.n)
        t2 = [emit_lntable(dve.n)]
        emit_attn(at_dst, t2, next_sums=("f2" in RUN_PHASES))
        if "f2" not in RUN_PHASES:
            SY.wait(dmo, dmo.n)
            SY.wait(sx[0], sx[0].n)
            SY.wait(sx[1], sx[1].n)
            SY.wait(sw[0], sw[0].n)
            SY.wait(sw[1], sw[1].n)
            SY.wait(sw[2], sw[2].n)
            SY.wait(dmw, dmw.n)
            SY.wait(dmc, dmc.n)
            SY.wait(dmg, dmg.n)
        flush_block()

    if "f2" in RUN_PHASES:
        ffn2 = ExitStack()
        wfc2 = sb("wfc_sb2", [128, 8 * HID], BF16, ffn2)
        wp2 = sb("wp_sb2", [128, 32 * D], BF16, ffn2)
        gT3_2 = (sb("gT3_2a", [128, G3SPLIT, TSB], BF16, ffn2),
                 sb("gT3_2b", [128, 32 - G3SPLIT, TSB], BF16, ffn2))
        w_gates = {}
        stripe_sems = [sw[0], sw[1], sw[2], dmg]
        for j in range(4):
            sj = stripe_sems[j]
            vj = dmain(SY, sj,
                       wfc2[:, :].rearrange("p (dc j n) -> p dc j n", dc=8, j=4)[:, :, j, :],
                       wspill[:, 0:32768].rearrange("p (dc j n) -> p dc j n", dc=8, j=4)[:, :, j, :])
            w_gates[8 * j] = (sj, vj)
        vr = dmain(SY, dmw, wp2[:, :], wspill[:, 32768:65536])
        SY.wait(dmo, dmo.n)
        t3 = [emit_lntable(dve.n)]
        emit_ffn(xs2, y_out, wfc2, wp2, gT3_2, t3, next_sums=False, spill=False,
                 w_gates=w_gates, wp_gate=(dmw, vr))
        SY.wait(dmo, dmo.n)
        SY.wait(sx[0], sx[0].n)
        SY.wait(sx[1], sx[1].n)
        SY.wait(sw[0], sw[0].n)
        SY.wait(sw[1], sw[1].n)
        SY.wait(sw[2], sw[2].n)
        SY.wait(dmw, dmw.n)
        SY.wait(dmc, dmc.n)
        SY.wait(dmg, dmg.n)
        flush_block()
        ffn2.close()

    es.close()
    return nc


def host_inputs(inputs):
    x = np.ascontiguousarray(inputs["L0_attn_hidden_states"], dtype=np.float32)
    wqk = np.ascontiguousarray(inputs["L0_attn_Wqk"], dtype=np.float32)
    bqk = np.ascontiguousarray(inputs["L0_attn_bqk"], dtype=np.float32)
    wv = np.ascontiguousarray(inputs["L0_attn_Wv"], dtype=np.float32)
    bv = np.ascontiguousarray(inputs["L0_attn_bv"], dtype=np.float32)
    wfc = np.ascontiguousarray(inputs["L0_ffn_W_fc"], dtype=np.float32)
    lnw = np.ascontiguousarray(inputs["L0_ffn_ln_w"], dtype=np.float32)
    lnb = np.ascontiguousarray(inputs["L0_ffn_ln_b"], dtype=np.float32)
    wp = np.ascontiguousarray(inputs["L0_ffn_W_proj"], dtype=np.float32)

    shared = {
        "wfcT": np.ascontiguousarray(wfc.T),
        "wpT": np.ascontiguousarray(wp.T),
        "wqk": wqk,
        "wv": wv,
        "bqk_pt": np.ascontiguousarray(bqk.reshape(16, 128).T),
        "lnw_pt": np.ascontiguousarray(lnw.reshape(32, 128).T),
        "lnb_pt": np.ascontiguousarray(lnb.reshape(32, 128).T),
        "bv_t": np.ascontiguousarray(np.broadcast_to(bv, (128, D))),
        "id_bf": np.eye(128, dtype=BFNP),
        "id_f32": np.eye(128, dtype=np.float32),
        "ones_col": np.ones((128, 1), dtype=BFNP),
        "ones_row": np.ones((1, 128), dtype=np.float32),
        "mask_ut": np.triu(np.ones((128, 128), dtype=np.float32)).astype(BFNP),
        "cst": np.broadcast_to(np.array(
            [EPS, EXP_SHIFT, 0.0, 1.0, float(np.log(0.5)), 0.5e-6, 0.0, 0.0],
            dtype=np.float32), (128, 8)).copy(),
    }
    maps = []
    for c in range(8):
        m = dict(shared)
        m["x_in"] = np.ascontiguousarray(x[2 * c:2 * c + 2].reshape(T, D))
        maps.append(m)
    return maps


def kernel(**inputs):
    maps = host_inputs(inputs)
    nc = build_program()
    res = run_bass_kernel_spmd(nc, maps, core_ids=list(range(8)))
    out = np.empty((16, SEQ, D), dtype=np.float32)
    for c in range(8):
        out[2 * c:2 * c + 2] = np.asarray(res.results[c]["y"]).reshape(NSEQ, SEQ, D)
    return out


# version 17
# speedup vs baseline: 1.0043x; 1.0043x over previous
import numpy as np
import ml_dtypes
from contextlib import ExitStack

import concourse.bass as bass
import concourse.mybir as mybir
from concourse.bass_utils import run_bass_kernel_spmd

F32 = mybir.dt.float32
BF16 = mybir.dt.bfloat16
AF = mybir.ActivationFunctionType
ALU = mybir.AluOpType
AX = mybir.AxisListType
BFNP = ml_dtypes.bfloat16

D = 1024
HID = 4096
SEQ = 2048
NSEQ = 2
T = NSEQ * SEQ
NCH = T // 128
TSB = 512
NSB = T // TSB
CPS = TSB // 128
NH = 16
HD = 64
NPAIR = 8
EPS = 1e-5
EXP_SHIFT = -16.0

RUN_PHASES = ("f1", "attn", "f2")
DEBUG_TAPS = False
DEV_NSB = None
DEV_SKIP = ()
DEV_RSTD_AFTER = "ci1"


class Sem:
    def __init__(self, es, nc, name):
        self.h = es.enter_context(nc.semaphore(name))
        self.n = 0

    def bump(self, k=1):
        self.n += k
        return self.n


class Eng:
    def __init__(self, name):
        self.name = name
        self.ops = []

    def __call__(self, fn):
        self.ops.append(fn)

    def wait(self, sem, val):
        if val <= 0:
            return
        assert val <= sem.n, f"{self.name} waits future {val} > {sem.n}"
        self.ops.append(lambda e, h=sem.h, v=val: e.wait_ge(h, v))


def build_program():
    nc = bass.Bass("TRN2", target_bir_lowering=False)
    es = ExitStack()

    dma = Sem(es, nc, "s_dma")
    dmr = Sem(es, nc, "s_dmr")
    sx = [Sem(es, nc, f"s_x{i}") for i in range(2)]
    sxr = [Sem(es, nc, f"s_xr{i}") for i in range(2)]
    sw = [Sem(es, nc, f"s_w{i}") for i in range(3)]
    dmw = Sem(es, nc, "s_dmw")
    dmc = Sem(es, nc, "s_dmc")
    dmo = Sem(es, nc, "s_dmo")
    dmg = Sem(es, nc, "s_dmg")
    pe = Sem(es, nc, "s_pe")
    act = Sem(es, nc, "s_act")
    dve = Sem(es, nc, "s_dve")
    pool = Sem(es, nc, "s_pool")

    SY = Eng("sync"); PE = Eng("tensor"); ACT = Eng("scalar")
    DVE = Eng("vector"); PO = Eng("gpsimd")

    def flush_block():
        with nc.Block() as block:
            for attr, eng in (("sync", SY), ("tensor", PE), ("scalar", ACT),
                              ("vector", DVE), ("gpsimd", PO)):
                if not eng.ops:
                    continue
                ops, eng.ops = eng.ops, []

                def body(e, ops=ops):
                    for fn in ops:
                        fn(e)
                getattr(block, attr)(body)

    x_in = nc.dram_tensor("x_in", [T, D], F32, kind="ExternalInput").ap()
    wfcT_h = nc.dram_tensor("wfcT", [D, HID], F32, kind="ExternalInput").ap()
    wpT_h = nc.dram_tensor("wpT", [HID, D], F32, kind="ExternalInput").ap()
    wqk_h = nc.dram_tensor("wqk", [D, 2 * D], F32, kind="ExternalInput").ap()
    wv_h = nc.dram_tensor("wv", [D, D], F32, kind="ExternalInput").ap()
    bqk_pt_h = nc.dram_tensor("bqk_pt", [128, 16], F32, kind="ExternalInput").ap()
    lnw_pt_h = nc.dram_tensor("lnw_pt", [128, 32], F32, kind="ExternalInput").ap()
    lnb_pt_h = nc.dram_tensor("lnb_pt", [128, 32], F32, kind="ExternalInput").ap()
    bv_t_h = nc.dram_tensor("bv_t", [128, D], F32, kind="ExternalInput").ap()
    id_bf_h = nc.dram_tensor("id_bf", [128, 128], BF16, kind="ExternalInput").ap()
    id_f32_h = nc.dram_tensor("id_f32", [128, 128], F32, kind="ExternalInput").ap()
    ones_col_h = nc.dram_tensor("ones_col", [128, 1], BF16, kind="ExternalInput").ap()
    ones_row_h = nc.dram_tensor("ones_row", [1, 128], F32, kind="ExternalInput").ap()
    mask_ut_h = nc.dram_tensor("mask_ut", [128, 128], BF16, kind="ExternalInput").ap()
    cst_h = nc.dram_tensor("cst", [128, 8], F32, kind="ExternalInput").ap()

    y_out = nc.dram_tensor("y", [T, D], F32, kind="ExternalOutput").ap()
    xs1 = nc.dram_tensor("xs1", [T, D], F32).ap()
    xs2 = nc.dram_tensor("xs2", [T, D], F32).ap()
    wspill = nc.dram_tensor("wspill", [128, 2 * 32768], BF16).ap()
    if DEBUG_TAPS:
        dbg_uT = nc.dram_tensor("dbg_uT", [128, 8 * TSB], BF16, kind="ExternalOutput").ap()
        dbg_gT = nc.dram_tensor("dbg_gT", [128, 32 * TSB], BF16, kind="ExternalOutput").ap()
        dbg_rows = nc.dram_tensor("dbg_rows", [128, 16], F32, kind="ExternalOutput").ap()
        dbg_w = nc.dram_tensor("dbg_w", [128, 2048], BF16, kind="ExternalOutput").ap()
        dbg_u = nc.dram_tensor("dbg_u", [128, 1024], BF16, kind="ExternalOutput").ap()
        dbg_st = nc.dram_tensor("dbg_st", [128, 10], F32, kind="ExternalOutput").ap()
        dbg_st2 = nc.dram_tensor("dbg_st2", [128, 10], F32, kind="ExternalOutput").ap()
        dbg_gT2 = nc.dram_tensor("dbg_gT2", [128, 32 * TSB], BF16, kind="ExternalOutput").ap()
        dbg_wf = nc.dram_tensor("dbg_wf", [128, 32768], BF16, kind="ExternalOutput").ap()
        dbg_wp = nc.dram_tensor("dbg_wp", [128, 32768], BF16, kind="ExternalOutput").ap()

    def sb(name, shape, dt, stack=es):
        return stack.enter_context(nc.sbuf_tensor(name, shape, dt))

    xin = [sb(f"xin{i}", [128, D], F32) for i in range(2)]
    xout = sb("xout", [128, D], F32)
    u_t = sb("u_t", [128, D], BF16)
    stat = [sb(f"stat{i}", [128, 10], F32) for i in range(2)]
    rows = sb("rows", [1, 3 * TSB], F32)
    negmurow = sb("negmurow", [1, TSB], BF16)
    rstd05 = sb("rstd05", [128, CPS], F32)
    cst = sb("cstv", [128, 8], F32)
    id_bf = sb("id_bf_s", [128, 128], BF16)
    ones_col = sb("ones_col_s", [128, 1], BF16)
    ones_row = sb("ones_row_s", [1, 128], F32)
    bqk_pt = sb("bqk_pt_s", [128, 16], F32)
    lnw_pt = sb("lnw_pt_s", [128, 32], F32)
    lnb_pt = sb("lnb_pt_s", [128, 32], F32)
    lnb_bf = sb("lnb_bf", [128, 32], BF16)
    bv_t = sb("bv_t_s", [128, D], BF16)
    fbias_t = sb("fbias_t", [128, D], BF16)
    wsum_row = sb("wsum_row", [1, D], BF16)
    uT3 = sb("uT3", [128, 8, TSB], BF16)
    xsq = sb("xsq", [128, D], BF16)
    sums_t = sb("sums_t", [128, 2 * NCH], F32)
    rstd_t = sb("rstd_t", [128, NCH], F32)
    negmu_t = sb("negmu_t", [128, NCH], F32)
    lnrows = sb("lnrows", [128, 2 * NCH], F32)

    nc.const_aps.register(F32, EPS, cst[:, 0:1])
    nc.const_aps.register(F32, EXP_SHIFT, cst[:, 1:2])
    nc.const_aps.register(F32, 0.0, cst[:, 2:3])
    nc.const_aps.register(F32, 1.0, cst[:, 3:4])
    nc.const_aps.register(F32, float(np.log(0.5)), cst[:, 4:5])
    nc.const_aps.register(F32, 0.5e-6, cst[:, 5:6])

    tpA = es.enter_context(nc.psum_tensor("tpA", [128, 512], BF16))
    tpB = es.enter_context(nc.psum_tensor("tpB", [128, 512], BF16))
    psA = es.enter_context(nc.psum_tensor("psA", [128, 1024], F32))
    psB = es.enter_context(nc.psum_tensor("psB", [128, 1024], F32))
    psC = es.enter_context(nc.psum_tensor("psC", [128, 1024], F32))

    def dmain(eng, sem, out_ap, in_ap):
        eng(lambda e, o=out_ap, i=in_ap: e.dma_start(out=o, in_=i).then_inc(sem.h, 16))
        return sem.bump(16)

    def r2(ap):
        return ap.rearrange("p a b -> p (a b)")

    tp_free = [0, 0]
    st_gate = [0, 0]

    def emit_ln_chunk(src_hbm, g, cbuf, xin_gate_dve, ut_gate_pe, uT_gate_pe,
                      table_done):
        SY.wait(dve, xin_gate_dve)
        vin = dmain(SY, sx[cbuf], xin[cbuf][:, :], src_hbm[128 * g:128 * (g + 1), :])
        DVE.wait(sx[cbuf], vin)
        DVE.wait(pe, ut_gate_pe)
        DVE.wait(act, table_done[0])
        DVE(lambda e, g=g: e.tensor_scalar(
            out=u_t[:, :], in0=xin[cbuf][:, :],
            scalar1=negmu_t[:, g:g + 1], scalar2=rstd_t[:, g:g + 1],
            op0=ALU.add, op1=ALU.mult).then_inc(dve.h, 1))
        u_done = dve.bump()
        ci = g % CPS
        for half in range(2):
            tp = tpA if half == 0 else tpB
            PE.wait(dve, u_done)
            PE.wait(dve, tp_free[half])
            for k in range(4):
                dcc = 4 * half + k
                if k == 3:
                    PE(lambda e, tp=tp, k=k, dcc=dcc: e.transpose(
                        out=tp[:, 128 * k:128 * (k + 1)],
                        in_=u_t[:, 128 * dcc:128 * (dcc + 1)],
                        identity=id_bf[:, :]).then_inc(pe.h, 1))
                else:
                    PE(lambda e, tp=tp, k=k, dcc=dcc: e.transpose(
                        out=tp[:, 128 * k:128 * (k + 1)],
                        in_=u_t[:, 128 * dcc:128 * (dcc + 1)],
                        identity=id_bf[:, :]))
            t_done = pe.bump()
            DVE.wait(pe, t_done)
            DVE.wait(pe, uT_gate_pe)
            DVE(lambda e, tp=tp, half=half, ci=ci: e.tensor_copy(
                out=uT3[:, 4 * half:4 * (half + 1), 128 * ci:128 * (ci + 1)],
                in_=tp[:, 0:512]).then_inc(dve.h, 1))
            tp_free[half] = dve.bump()
        return u_done, tp_free[1]

    def emit_sums(src_tile, g):
        DVE(lambda e: e.tensor_tensor(out=xsq[:, :], in0=src_tile[:, :],
                                      in1=src_tile[:, :], op=ALU.mult))
        DVE(lambda e, g=g: e.reduce_sum(out=sums_t[:, 2 * g + 1:2 * g + 2],
                                        in_=xsq[:, :], axis=AX.X))
        DVE(lambda e, g=g: e.reduce_sum(out=sums_t[:, 2 * g:2 * g + 1],
                                        in_=src_tile[:, :],
                                        axis=AX.X).then_inc(dve.h, 1))
        return dve.bump()

    def emit_lntable(sums_done_dve):
        W = NCH
        sx_ap = sums_t[:, :].rearrange("p (g two) -> p two g", two=2)
        ACT.wait(dve, sums_done_dve)
        ACT(lambda e: e.activation(out=negmu_t[:, :], in_=sx_ap[:, 0:1, :].rearrange("p a g -> p (a g)"),
                                   func=AF.Identity, scale=-1.0 / D))
        ACT(lambda e: e.activation(out=lnrows[:, 0:W], in_=sx_ap[:, 0:1, :].rearrange("p a g -> p (a g)"),
                                   func=AF.Square, scale=1.0 / D))
        ACT(lambda e: e.activation(out=lnrows[:, W:2 * W], in_=sx_ap[:, 1:2, :].rearrange("p a g -> p (a g)"),
                                   func=AF.Identity, scale=1.0 / D).then_inc(act.h, 1))
        aff = act.bump()
        DVE.wait(act, aff)
        DVE(lambda e: e.tensor_tensor(out=lnrows[:, W:2 * W], in0=lnrows[:, W:2 * W],
                                      in1=lnrows[:, 0:W],
                                      op=ALU.subtract).then_inc(dve.h, 1))
        var_done = dve.bump()
        ACT.wait(dve, var_done)
        ACT(lambda e: e.activation(out=lnrows[:, 0:W], in_=lnrows[:, W:2 * W],
                                   func=AF.Ln, bias=EPS, scale=1.0).then_inc(act.h, 1))
        lnv = act.bump()
        DVE.wait(act, lnv)
        DVE(lambda e: e.tensor_copy(out=lnrows[:, W:2 * W],
                                    in_=lnrows[:, 0:W]).then_inc(dve.h, 1))
        cp = dve.bump()
        ACT.wait(dve, cp)
        ACT(lambda e: e.activation(out=rstd_t[:, :], in_=lnrows[:, W:2 * W],
                                   func=AF.Exp, scale=-0.5).then_inc(act.h, 1))
        return act.bump()

    for o, i in ((cst, cst_h), (id_bf, id_bf_h),
                 (ones_col, ones_col_h), (ones_row, ones_row_h),
                 (bqk_pt, bqk_pt_h), (lnw_pt, lnw_pt_h), (lnb_pt, lnb_pt_h)):
        dmain(SY, dmc, o[:, :], i)
    dmain(PO, dmg, bv_t[:, :], bv_t_h)
    consts_done = dmc.n

    DVE.wait(dmc, consts_done)
    DVE(lambda e: e.tensor_copy(out=lnb_bf[:, :], in_=lnb_pt[:, :]).then_inc(dve.h, 1))
    dve.bump()

    ffn_stack = ExitStack()
    wfc_sb = sb("wfc_sb", [128, 8 * HID], BF16, ffn_stack)
    wp_sb = sb("wp_sb", [128, 32 * D], BF16, ffn_stack)

    G3SPLIT = 8
    gT3_1a = sb("gT3a", [128, G3SPLIT, TSB], BF16, ffn_stack)
    prep_stack = ExitStack()
    id_f32 = sb("id_f32_s", [128, 128], F32, prep_stack)
    idf_v = dmain(SY, dmw, id_f32[:, :], id_f32_h)
    biasrow = sb("biasrow", [1, D], BF16, prep_stack)
    thr = sb("thr", [128, 4], F32, prep_stack)
    absacc = sb("absacc", [128, 66], F32, prep_stack)
    gam = sb("gam", [1, 8], F32, prep_stack)
    wstage = [sb(f"wstage{i}", [128, 1024], F32, prep_stack) for i in range(3)]
    sgn_b = [sb(f"sgn{i}", [128, 1024], BF16, prep_stack) for i in range(2)]
    abs_b = [sb(f"absb{i}", [128, 1024], F32, prep_stack) for i in range(2)]
    msk_b = [sb(f"mskb{i}", [128, 1024], BF16, prep_stack) for i in range(2)]
    onesrow_bf = sb("onesrow_bf", [1, 128], BF16, prep_stack)

    def wtile_ap(which, i):
        if which == 0:
            dc, j = divmod(i, 4)
            return wfcT_h[128 * dc:128 * (dc + 1), 1024 * j:1024 * (j + 1)]
        return wpT_h[128 * i:128 * (i + 1), :]

    def wtile_tgt(which, i):
        if which == 0:
            dc, j = divmod(i, 4)
            return wfc_sb[:, dc * HID + 1024 * j: dc * HID + 1024 * (j + 1)]
        return wp_sb[:, 1024 * i: 1024 * (i + 1)]

    stage_rd = [0, 0, 0]
    prepass_rd = [0, 0]
    sums_last = 0
    for which in (0, 1):
        for i in range(32):
            b = (which * 32 + i) % 3
            SY.wait(dve, stage_rd[b])
            v = dmain(SY, sw[b], wstage[b][:, :], wtile_ap(which, i))
            DVE.wait(sw[b], v)
            DVE(lambda e, b=b, col=which * 32 + i + 2: e.reduce_sum(
                out=absacc[:, col:col + 1], in_=wstage[b][:, :], axis=AX.X,
                apply_absolute_value=True).then_inc(dve.h, 1))
            stage_rd[b] = dve.bump()
            if which == 0:
                cbuf = i % 2
                SY.wait(dve, prepass_rd[cbuf])
                vin = dmain(SY, sx[cbuf], xin[cbuf][:, :],
                            x_in[128 * i:128 * (i + 1), :])
                DVE.wait(sx[cbuf], vin)
                sums_last = emit_sums(xin[cbuf], i)
                prepass_rd[cbuf] = sums_last
    DVE(lambda e: e.memset(onesrow_bf[0:1, :], 1.0).then_inc(dve.h, 1))
    p1_done = dve.bump()
    ACT.wait(dve, p1_done)
    ACT(lambda e: e.nop().then_inc(act.h, 1))
    p1_echo = act.bump()
    DVE.wait(act, p1_echo)
    for w in (0, 1):
        DVE(lambda e, w=w: e.reduce_sum(
            out=absacc[:, w:w + 1], in_=absacc[:, 32 * w + 2:32 * w + 34], axis=AX.X))
    DVE(lambda e: e.tensor_copy(out=gam[0:1, 4:5], in_=gam[0:1, 4:5]).then_inc(dve.h, 1))
    abscols_done = dve.bump()
    PE.wait(dve, abscols_done)
    PE.wait(dmw, idf_v)
    PE(lambda e: e.transpose(out=psC[0:1, 0:128], in_=absacc[:, 0:1],
                             identity=id_f32[:, :]))
    PE(lambda e: e.transpose(out=psC[0:1, 128:256], in_=absacc[:, 1:2],
                             identity=id_f32[:, :]).then_inc(pe.h, 1))
    tr_done = pe.bump()
    DVE.wait(pe, tr_done)
    for w in (0, 1):
        DVE(lambda e, w=w: e.reduce_sum(out=gam[0:1, w:w + 1],
                                        in_=psC[0:1, 128 * w:128 * (w + 1)], axis=AX.X))
    DVE(lambda e: e.tensor_copy(out=gam[0:1, 6:8], in_=gam[0:1, 0:2]).then_inc(dve.h, 1))
    sums_g = dve.bump()
    ACT.wait(dve, sums_g)
    ACT(lambda e: e.activation(out=gam[0:1, 2:4], in_=gam[0:1, 0:2], func=AF.Identity,
                               scale=1.0 / (2.0 * HID * D),
                               bias=0.5e-6).then_inc(act.h, 1))
    thr_row = act.bump()
    PE.wait(act, thr_row)
    PE(lambda e: e.matmul(out=psC[:, 4:6], lhsT=ones_row[0:1, :], rhs=gam[0:1, 2:4],
                          start=True, stop=True).then_inc(pe.h, 1))
    thr_rep = pe.bump()
    ACT.wait(pe, thr_rep)
    ACT(lambda e: e.activation(out=thr[:, 0:1], in_=psC[:, 4:5], func=AF.Identity))
    ACT(lambda e: e.activation(out=thr[:, 1:2], in_=psC[:, 4:5], func=AF.Identity,
                               scale=-1.0))
    ACT(lambda e: e.activation(out=thr[:, 2:3], in_=psC[:, 5:6], func=AF.Identity))
    ACT(lambda e: e.activation(out=thr[:, 3:4], in_=psC[:, 5:6], func=AF.Identity,
                               scale=-1.0).then_inc(act.h, 1))
    thr_done_act = act.bump()
    thr_done = None
    t1 = [emit_lntable(sums_last)]

    stage_rd2 = [(dve, stage_rd[0]), (dve, stage_rd[1]), (dve, stage_rd[2])]
    buf_rd = [0, 0]
    msk_rd = [0, 0]
    wfc_jgate = {}
    kctr = [0]
    for which in (1, 0):
        for idx in range(32):
            i = idx if which == 1 else (idx % 8) * 4 + idx // 8
            b = kctr[0] % 3
            kctr[0] += 1
            lb = idx % 2
            tgt = wtile_tgt(which, i)
            SY.wait(*stage_rd2[b])
            v = dmain(SY, sw[b], wstage[b][:, :], wtile_ap(which, i))
            ACT.wait(sw[b], v)
            ACT.wait(pool, buf_rd[lb])
            ACT(lambda e, b=b, lb=lb: e.activation(
                out=sgn_b[lb][:, :], in_=wstage[b][:, :], func=AF.Sign))
            ACT(lambda e, b=b, lb=lb: e.activation(
                out=abs_b[lb][:, :], in_=wstage[b][:, :],
                func=AF.Abs).then_inc(act.h, 1))
            sa_done = act.bump()
            stage_rd2[b] = (act, sa_done)
            DVE.wait(act, sa_done)
            DVE.wait(act, thr_done_act)
            DVE.wait(pool, msk_rd[lb])
            DVE(lambda e, lb=lb, w=which: e.tensor_scalar(
                out=msk_b[lb][:, :], in0=abs_b[lb][:, :],
                scalar1=thr[:, 2 * w:2 * w + 1], scalar2=None,
                op0=ALU.is_gt, op1=ALU.bypass).then_inc(dve.h, 1))
            mk_done = dve.bump()
            PO.wait(dve, mk_done)
            PO(lambda e, lb=lb, tgt=tgt: e.tensor_tensor(
                out=tgt, in0=sgn_b[lb][:, :], in1=msk_b[lb][:, :],
                op=ALU.mult).then_inc(pool.h, 1))
            wq_done = pool.bump()
            buf_rd[lb] = wq_done
            msk_rd[lb] = wq_done
            if which == 0 and idx % 8 == 7:
                wfc_jgate[8 * (idx // 8)] = wq_done
            if which == 1:
                PE.wait(pool, wq_done)
                PE(lambda e, i=i, tgt=tgt: e.matmul(
                    out=psA[0:1, 0:512], lhsT=lnb_bf[:, i:i + 1], rhs=tgt[:, 0:512],
                    start=(i == 0), stop=(i == 31)))
                PE(lambda e, i=i, tgt=tgt: e.matmul(
                    out=psA[0:1, 512:1024], lhsT=lnb_bf[:, i:i + 1],
                    rhs=tgt[:, 512:1024], start=(i == 0),
                    stop=(i == 31)).then_inc(pe.h, 1))
                br_mm = pe.bump()
                DVE.wait(pe, br_mm)
                DVE(lambda e, tgt=tgt, i=i: e.tensor_scalar(
                    out=tgt, in0=tgt, scalar1=lnw_pt[:, i:i + 1], scalar2=None,
                    op0=ALU.mult, op1=ALU.bypass).then_inc(dve.h, 1))
                sc_done = dve.bump()
                PE.wait(dve, sc_done)
                PE(lambda e, i=i, tgt=tgt: e.matmul(
                    out=psB[0:1, 0:512], lhsT=ones_col[:, 0:1], rhs=tgt[:, 0:512],
                    start=(i == 0), stop=(i == 31)))
                PE(lambda e, i=i, tgt=tgt: e.matmul(
                    out=psB[0:1, 512:1024], lhsT=ones_col[:, 0:1],
                    rhs=tgt[:, 512:1024], start=(i == 0),
                    stop=(i == 31)).then_inc(pe.h, 1))
                pe.bump()
    wsum_mm_done = pe.n
    wp_pool_last = pool.n
    DVE.wait(pe, wsum_mm_done)
    DVE(lambda e: e.tensor_scalar(out=biasrow[0:1, :], in0=psA[0:1, :], scalar1=0.5,
                                  scalar2=None, op0=ALU.mult, op1=ALU.bypass))
    DVE(lambda e: e.tensor_copy(out=wsum_row[0:1, :], in_=psB[0:1, :]).then_inc(dve.h, 1))
    rows_done = dve.bump()
    PE.wait(dve, rows_done)
    PE(lambda e: e.matmul(out=psA[:, 0:512], lhsT=onesrow_bf[0:1, :],
                          rhs=biasrow[0:1, 0:512], start=True, stop=True))
    PE(lambda e: e.matmul(out=psA[:, 512:1024], lhsT=onesrow_bf[0:1, :],
                          rhs=biasrow[0:1, 512:1024], start=True,
                          stop=True).then_inc(pe.h, 1))
    fb_mm = pe.bump()
    DVE.wait(pe, fb_mm)
    DVE(lambda e: e.tensor_copy(out=fbias_t[:, :], in_=psA[:, :]).then_inc(dve.h, 1))
    prep_dve_last = dve.bump()

    prep_stack.close()
    gT3_1 = (gT3_1a, sb("gT3b", [128, 32 - G3SPLIT, TSB], BF16, ffn_stack))

    def emit_ffn(src_hbm, dst_hbm, wfc_t, wp_t, gT3, table_done, next_sums, spill=False, xin_seed=(0, 0), w_gates=None, wp_gate=None, spill_gate=None, g3b_gate=None):
        NSBr = DEV_NSB if DEV_NSB else NSB
        ffn_local = ExitStack()
        tag = "a" if spill else "b"
        xres = [sb(f"xres{tag}{i}", [128, D], F32, ffn_local) for i in range(2)]

        def g3(h0, h1, c0=0, c1=TSB):
            if h1 <= G3SPLIT:
                return gT3[0][:, h0:h1, c0:c1]
            assert h0 >= G3SPLIT, (h0, h1)
            return gT3[1][:, h0 - G3SPLIT:h1 - G3SPLIT, c0:c1]
        spill_done = 0
        u_done_ms = {-2: xin_seed[0], -1: xin_seed[1]}
        uT_ready_sb = {}
        stt_done = {}
        psum_free = {}
        out_done = {}
        sqmm_last = [0]
        mm2_last = 0
        mm1_last = 0
        stats_pe_last = [0]

        def ln_front(g):
            cbuf = g % 2
            SY.wait(dve, u_done_ms.get(g - 2, 0))
            vin = dmain(SY, sx[cbuf], xin[cbuf][:, :],
                        src_hbm[128 * g:128 * (g + 1), :])
            DVE.wait(sx[cbuf], vin)
            DVE.wait(act, table_done[0])
            DVE(lambda e, g=g: e.tensor_scalar(
                out=u_t[:, :], in0=xin[cbuf][:, :],
                scalar1=negmu_t[:, g:g + 1], scalar2=rstd_t[:, g:g + 1],
                op0=ALU.add, op1=ALU.mult).then_inc(dve.h, 1))
            u_done_ms[g] = dve.bump()

        def ln_transpose(g):
            ci = g % CPS
            for half in range(2):
                tp = tpA if half == 0 else tpB
                PE.wait(dve, u_done_ms[g])
                PE.wait(dve, tp_free[half])
                for k in range(4):
                    dcc = 4 * half + k
                    if k == 3:
                        PE(lambda e, tp=tp, k=k, dcc=dcc: e.transpose(
                            out=tp[:, 128 * k:128 * (k + 1)],
                            in_=u_t[:, 128 * dcc:128 * (dcc + 1)],
                            identity=id_bf[:, :]).then_inc(pe.h, 1))
                    else:
                        PE(lambda e, tp=tp, k=k, dcc=dcc: e.transpose(
                            out=tp[:, 128 * k:128 * (k + 1)],
                            in_=u_t[:, 128 * dcc:128 * (dcc + 1)],
                            identity=id_bf[:, :]))
                t_done = pe.bump()
                DVE.wait(pe, t_done)
                DVE.wait(pe, mm1_last)
                DVE(lambda e, tp=tp, half=half, ci=ci: e.tensor_copy(
                    out=uT3[:, 4 * half:4 * (half + 1), 128 * ci:128 * (ci + 1)],
                    in_=tp[:, 0:512]).then_inc(dve.h, 1))
                tp_free[half] = dve.bump()
            uT_ready_sb[g // CPS] = tp_free[1]

        for ci in range(CPS):
            ln_front(ci)
            ln_transpose(ci)
        if spill:
            if spill_gate:
                SY.wait(*spill_gate)
            dmain(SY, dmw, wspill[:, 0:32768], wfc_t[:, :])
            spill_done = dmain(SY, dmw, wspill[:, 32768:65536], wp_t[:, :])

        for sbi in range(NSBr):
            hp_free = [0, 0, 0, 0]
            PE.wait(dve, uT_ready_sb[sbi])
            PE.wait(dve, psum_free.get(sbi * CPS - 2, 0))
            gelu_last = 0
            gelu_psA_last = 0
            for hf in range(32):
                pr = hf % 4
                tgt = psA if pr < 2 else psB
                lo = 512 * (pr % 2)
                if hf == 2:
                    PE.wait(dve, psum_free.get(sbi * CPS - 1, 0))
                if sbi == 0 and w_gates and hf in w_gates:
                    gv = w_gates[hf]
                    for sv in (gv if isinstance(gv, list) else [gv]):
                        PE.wait(*sv)
                PE.wait(act, hp_free[pr])
                for dc in range(8):
                    if dc == 7:
                        PE(lambda e, hf=hf, dc=dc, tgt=tgt, lo=lo: e.matmul(
                            out=tgt[:, lo:lo + 512],
                            lhsT=wfc_t[:, dc * HID + 128 * hf:dc * HID + 128 * (hf + 1)],
                            rhs=r2(uT3[:, dc:dc + 1, :]),
                            start=False, stop=True).then_inc(pe.h, 1))
                    else:
                        PE(lambda e, hf=hf, dc=dc, tgt=tgt, lo=lo: e.matmul(
                            out=tgt[:, lo:lo + 512],
                            lhsT=wfc_t[:, dc * HID + 128 * hf:dc * HID + 128 * (hf + 1)],
                            rhs=r2(uT3[:, dc:dc + 1, :]),
                            start=(dc == 0), stop=False))
                mm1_done = pe.bump()
                if hf % 2 == 1:
                    ACT.wait(pe, mm1_done)
                    if sbi == 0 and hf == G3SPLIT + 1 and g3b_gate:
                        ACT.wait(*g3b_gate)
                    if sbi > 0 and hf == 1:
                        ACT.wait(pe, mm2_last)
                    ACT(lambda e, hf=hf, tgt=tgt: e.activation(
                        out=r2(g3(hf - 1, hf + 1)), in_=tgt[:, :],
                        func=AF.Gelu).then_inc(act.h, 1))
                    gv = act.bump()
                    gelu_last = gv
                    if pr < 2:
                        gelu_psA_last = gv
                    hp_free[pr] = gv
                    hp_free[pr - 1] = gv
                    PE.wait(act, gv)
                    for k in range(2):
                        PE(lambda e, hf=hf, k=k: e.matmul(
                            out=psC[0:1, 0:512], lhsT=ones_col[:, 0:1],
                            rhs=r2(g3(hf - 1 + k, hf + k)),
                            start=(hf == 1 and k == 0), stop=(hf == 31 and k == 1)))
                    for k in range(2):
                        DVE.wait(act, gv)
                        DVE.wait(pe, sqmm_last[0])
                        DVE(lambda e, hf=hf, k=k: e.tensor_tensor(
                            out=u_t[:, 0:512],
                            in0=r2(g3(hf - 1 + k, hf + k)),
                            in1=r2(g3(hf - 1 + k, hf + k)),
                            op=ALU.mult).then_inc(dve.h, 1))
                        gq = dve.bump()
                        PE.wait(dve, gq)
                        PE(lambda e, hf=hf, k=k: e.matmul(
                            out=psC[0:1, 512:1024], lhsT=ones_col[:, 0:1],
                            rhs=u_t[:, 0:512],
                            start=(hf == 1 and k == 0),
                            stop=(hf == 31 and k == 1)).then_inc(pe.h, 1))
                        sqmm_last[0] = pe.bump()
            stats_pe_last[0] = pe.n
            mm1_last = pe.n
            DVE.wait(pe, stats_pe_last[0])
            DVE(lambda e: e.tensor_scalar(out=rows[0:1, 0:TSB], in0=psC[0:1, 0:512],
                                          scalar1=1.0 / HID, scalar2=None,
                                          op0=ALU.mult, op1=ALU.bypass))
            if mm2_last:
                DVE.wait(pe, mm2_last)
            DVE(lambda e: e.tensor_scalar(out=negmurow[0:1, :], in0=psC[0:1, 0:512],
                                          scalar1=-1.0 / HID, scalar2=None,
                                          op0=ALU.mult, op1=ALU.bypass))
            DVE(lambda e: e.tensor_scalar(out=rows[0:1, TSB:2 * TSB],
                                          in0=psC[0:1, 512:1024],
                                          scalar1=1.0 / HID, scalar2=None,
                                          op0=ALU.mult,
                                          op1=ALU.bypass).then_inc(dve.h, 1))
            rows_a = dve.bump()
            ACT.wait(dve, rows_a)
            ACT(lambda e: e.activation(out=rows[0:1, 2 * TSB:3 * TSB],
                                       in_=rows[0:1, 0:TSB],
                                       func=AF.Square).then_inc(act.h, 1))
            musq_b = act.bump()
            DVE.wait(act, musq_b)
            DVE(lambda e: e.tensor_tensor(out=rows[0:1, TSB:2 * TSB],
                                          in0=rows[0:1, TSB:2 * TSB],
                                          in1=rows[0:1, 2 * TSB:3 * TSB],
                                          op=ALU.subtract).then_inc(dve.h, 1))
            var_row = dve.bump()
            ACT.wait(dve, var_row)
            ACT(lambda e: e.activation(out=rows[0:1, 0:TSB],
                                       in_=rows[0:1, TSB:2 * TSB], func=AF.Ln,
                                       bias=EPS, scale=1.0).then_inc(act.h, 1))
            lnv_row = act.bump()
            DVE.wait(act, lnv_row)
            DVE(lambda e: e.tensor_copy(out=rows[0:1, 2 * TSB:3 * TSB],
                                        in_=rows[0:1, 0:TSB]).then_inc(dve.h, 1))
            cp_row = dve.bump()
            ACT.wait(dve, cp_row)
            ACT(lambda e: e.activation(out=rows[0:1, TSB:2 * TSB],
                                       in_=rows[0:1, 2 * TSB:3 * TSB], func=AF.Exp,
                                       scale=-0.5,
                                       bias=float(np.log(0.5))).then_inc(act.h, 1))
            rstd_row = act.bump()
            if DEV_RSTD_AFTER == "before":
                PE.wait(act, rstd_row)
                for cc in range(CPS):
                    if cc == CPS - 1:
                        PE(lambda e, cc=cc: e.matmul(
                            out=psC[:, 8 + cc:9 + cc],
                            lhsT=rows[0:1, TSB + 128 * cc:TSB + 128 * (cc + 1)],
                            rhs=ones_row[0:1, 0:1], start=True,
                            stop=True).then_inc(pe.h, 1))
                    else:
                        PE(lambda e, cc=cc: e.matmul(
                            out=psC[:, 8 + cc:9 + cc],
                            lhsT=rows[0:1, TSB + 128 * cc:TSB + 128 * (cc + 1)],
                            rhs=ones_row[0:1, 0:1], start=True, stop=True))
                rstdT = pe.bump()
                ACT.wait(pe, rstdT)
                if stt_done:
                    ACT.wait(dve, stt_done[sbi * CPS - 1])
                ACT(lambda e: e.activation(out=rstd05[:, :], in_=psC[:, 8:8 + CPS],
                                           func=AF.Identity).then_inc(act.h, 1))
                rstd05_done = act.bump()
            mm2_done = {}
            for ci in range(CPS):
                g = sbi * CPS + ci
                fps = psA if ci % 2 == 0 else psB
                PE.wait(dve, psum_free.get(g - 2, 0))
                if ci == 0:
                    PE.wait(act, gelu_psA_last)
                    if sbi == 0 and wp_gate:
                        PE.wait(*wp_gate)
                if ci == 1:
                    PE.wait(act, gelu_last)
                for h2 in range(32):
                    for half in range(2):
                        PE(lambda e, h2=h2, half=half, ci=ci, fps=fps: e.matmul(
                            out=fps[:, 512 * half:512 * (half + 1)],
                            lhsT=r2(g3(h2, h2 + 1, 128 * ci, 128 * (ci + 1))),
                            rhs=wp_t[:, 1024 * h2 + 512 * half:1024 * h2 + 512 * (half + 1)],
                            start=(h2 == 0), stop=False))
                if ci == 0:
                    PE.wait(dve, rows_a)
                for half in range(2):
                    if half == 1:
                        PE(lambda e, half=half, ci=ci, fps=fps: e.matmul(
                            out=fps[:, 512 * half:512 * (half + 1)],
                            lhsT=negmurow[0:1, 128 * ci:128 * (ci + 1)],
                            rhs=wsum_row[0:1, 512 * half:512 * (half + 1)],
                            start=False, stop=True).then_inc(pe.h, 1))
                    else:
                        PE(lambda e, half=half, ci=ci, fps=fps: e.matmul(
                            out=fps[:, 512 * half:512 * (half + 1)],
                            lhsT=negmurow[0:1, 128 * ci:128 * (ci + 1)],
                            rhs=wsum_row[0:1, 512 * half:512 * (half + 1)],
                            start=False, stop=True))
                mm2_done[ci] = pe.bump()
                if ci == CPS - 1:
                    mm2_last = mm2_done[ci]
                if DEV_RSTD_AFTER == "ci1" and ci == 0:
                    PE.wait(act, rstd_row)
                    for cc in range(CPS):
                        if cc == CPS - 1:
                            PE(lambda e, cc=cc: e.matmul(
                                out=psC[:, 8 + cc:9 + cc],
                                lhsT=rows[0:1, TSB + 128 * cc:TSB + 128 * (cc + 1)],
                                rhs=ones_row[0:1, 0:1], start=True,
                                stop=True).then_inc(pe.h, 1))
                        else:
                            PE(lambda e, cc=cc: e.matmul(
                                out=psC[:, 8 + cc:9 + cc],
                                lhsT=rows[0:1, TSB + 128 * cc:TSB + 128 * (cc + 1)],
                                rhs=ones_row[0:1, 0:1], start=True, stop=True))
                    rstdT = pe.bump()
                    ACT.wait(pe, rstdT)
                    if stt_done:
                        ACT.wait(dve, stt_done[sbi * CPS - 1])
                    ACT(lambda e: e.activation(out=rstd05[:, :], in_=psC[:, 8:8 + CPS],
                                               func=AF.Identity).then_inc(act.h, 1))
                    rstd05_done = act.bump()
                if (ci == 1 and not DEV_RSTD_AFTER) or (DEV_RSTD_AFTER and ci == CPS - 1 and False):
                    PE.wait(act, rstd_row)
                    for cc in range(CPS):
                        if cc == CPS - 1:
                            PE(lambda e, cc=cc: e.transpose(
                                out=psC[:, 8 + cc:9 + cc],
                                in_=rows[0:1, TSB + 128 * cc:TSB + 128 * (cc + 1)],
                                identity=id_f32[0:1, 0:1]).then_inc(pe.h, 1))
                        else:
                            PE(lambda e, cc=cc: e.transpose(
                                out=psC[:, 8 + cc:9 + cc],
                                in_=rows[0:1, TSB + 128 * cc:TSB + 128 * (cc + 1)],
                                identity=id_f32[0:1, 0:1]))
                    rstdT = pe.bump()
                    ACT.wait(pe, rstdT)
                    if stt_done:
                        ACT.wait(dve, stt_done[sbi * CPS - 1])
                    ACT(lambda e: e.activation(out=rstd05[:, :], in_=psC[:, 8:8 + CPS],
                                               func=AF.Identity).then_inc(act.h, 1))
                    rstd05_done = act.bump()
                if sbi + 1 < NSBr:
                    ln_front(g + CPS)
                    ln_transpose(g + CPS)
                SY.wait(dve, psum_free.get(g - 2, 0))
                vres = dmain(SY, sxr[ci % 2], xres[ci % 2][:, :],
                             src_hbm[128 * g:128 * (g + 1), :])
                DVE.wait(pe, mm2_done[ci])
                DVE.wait(sxr[ci % 2], vres)
                DVE.wait(dmo, out_done.get(g - 1, 0))
                if ci == 0:
                    DVE.wait(act, rstd05_done)
                DVE(lambda e, ci=ci, fps=fps: e.scalar_tensor_tensor(
                    out=xout[:, :], in0=fps[:, :], scalar=rstd05[:, ci:ci + 1],
                    in1=xres[ci % 2][:, :], op0=ALU.mult,
                    op1=ALU.add).then_inc(dve.h, 1))
                psum_free[g] = dve.bump()
                DVE(lambda e: e.tensor_tensor(
                    out=xout[:, :], in0=xout[:, :], in1=fbias_t[:, :],
                    op=ALU.add).then_inc(dve.h, 1))
                stt_done[g] = dve.bump()
                SY.wait(dve, stt_done[g])
                out_done[g] = dmain(SY, dmo, dst_hbm[128 * g:128 * (g + 1), :],
                                    xout[:, :])
                if next_sums:
                    stt_done[g] = emit_sums(xout, g)
            if DEV_RSTD_AFTER is True:
                PE.wait(act, rstd_row)
                for cc in range(CPS):
                    if cc == CPS - 1:
                        PE(lambda e, cc=cc: e.matmul(
                            out=psC[:, 8 + cc:9 + cc],
                            lhsT=rows[0:1, TSB + 128 * cc:TSB + 128 * (cc + 1)],
                            rhs=ones_row[0:1, 0:1], start=True,
                            stop=True).then_inc(pe.h, 1))
                    else:
                        PE(lambda e, cc=cc: e.matmul(
                            out=psC[:, 8 + cc:9 + cc],
                            lhsT=rows[0:1, TSB + 128 * cc:TSB + 128 * (cc + 1)],
                            rhs=ones_row[0:1, 0:1], start=True, stop=True))
                rstdT = pe.bump()
                ACT.wait(pe, rstdT)
                if stt_done:
                    ACT.wait(dve, stt_done[sbi * CPS - 1])
                ACT(lambda e: e.activation(out=rstd05[:, :], in_=psC[:, 8:8 + CPS],
                                           func=AF.Identity).then_inc(act.h, 1))
                rstd05_done = act.bump()
        if spill:
            SY.wait(dmw, spill_done)
        ffn_local.close()

    def emit_attn(at_dst, table_done, next_sums=True):
        at_stack = ExitStack()
        qkT3 = sb("qkT3", [128, 16, SEQ], BF16, at_stack)
        vbuf = sb("vbuf", [128, 16, 16, 65], BF16, at_stack)
        o_st = sb("o_st", [128, 4, D], BF16, at_stack)
        mask_ut = sb("mask_ut_s", [128, 128], BF16, at_stack)
        mv = dmain(SY, dmc, mask_ut[:, :], mask_ut_h)
        DVE(lambda e: e.memset(vbuf[:, :, :, 64:65], 1.0).then_inc(dve.h, 1))
        dve.bump()

        qk_read_pe = 0
        v_read_pe = 0
        oev_prev = 0
        mm_prev_sb = 0
        stats_prev_sb = 0
        xin_carry = [0, 0]
        for s in range(NSEQ):
            qw_stack = ExitStack()
            wqk_sb = sb(f"wqk_sb{s}", [128, 8 * 2048], BF16, qw_stack)
            wv_sb = sb(f"wv_sb{s}", [128, 8 * 1024], BF16, qw_stack)
            PO.wait(pe, v_read_pe)
            wq_v = dmain(PO, dmg, wqk_sb[:, :].rearrange("p (c n) -> p c n", n=2048),
                         wqk_h.rearrange("(c p) n -> p c n", p=128))
            wq_v = dmain(PO, dmg, wv_sb[:, :].rearrange("p (c n) -> p c n", n=1024),
                         wv_h.rearrange("(c p) n -> p c n", p=128))
            xin_free = {-2: xin_carry[0], -1: xin_carry[1]}
            qk_done = 0
            v_evac = 0
            first_pe_wait = True
            for sbq in range(4):
                uT_ready = 0
                for ci in range(CPS):
                    gl = 4 * sbq + ci
                    g = 16 * s + gl
                    u_done, uT_ready = emit_ln_chunk(
                        xs1, g, gl % 2, xin_free.get(gl - 2, 0),
                        stats_prev_sb, mm_prev_sb, table_done)
                    xin_free[gl] = u_done
                qp_free = [0, 0, 0, 0]
                PE.wait(dve, uT_ready)
                if first_pe_wait:
                    PE.wait(dmg, wq_v)
                    if s == 1:
                        PE.wait(dve, oev_prev)
                    first_pe_wait = False
                for fc in range(16):
                    pr = fc % 4
                    tgt = psA if pr < 2 else psB
                    lo = 512 * (pr % 2)
                    PE.wait(act, qp_free[pr])
                    for dc in range(8):
                        if dc == 7:
                            PE(lambda e, fc=fc, dc=dc, tgt=tgt, lo=lo: e.matmul(
                                out=tgt[:, lo:lo + 512],
                                lhsT=wqk_sb[:, dc * 2048 + 128 * fc:dc * 2048 + 128 * (fc + 1)],
                                rhs=r2(uT3[:, dc:dc + 1, :]),
                                start=False, stop=True).then_inc(pe.h, 1))
                        else:
                            PE(lambda e, fc=fc, dc=dc, tgt=tgt, lo=lo: e.matmul(
                                out=tgt[:, lo:lo + 512],
                                lhsT=wqk_sb[:, dc * 2048 + 128 * fc:dc * 2048 + 128 * (fc + 1)],
                                rhs=r2(uT3[:, dc:dc + 1, :]),
                                start=(dc == 0), stop=False))
                    qmm = pe.bump()
                    ACT.wait(pe, qmm)
                    if s == 1 and sbq == 0 and fc == 0:
                        ACT.wait(pe, qk_read_pe)
                    ACT(lambda e, fc=fc, sbq=sbq, tgt=tgt, lo=lo: e.activation(
                        out=r2(qkT3[:, fc:fc + 1, 512 * sbq:512 * (sbq + 1)]),
                        in_=tgt[:, lo:lo + 512], func=AF.Identity,
                        bias=bqk_pt[:, fc:fc + 1], scale=1.0).then_inc(act.h, 1))
                    qp_free[pr] = act.bump()
                qk_done = qp_free[(15) % 4]
                for ci in range(CPS):
                    tc = 4 * sbq + ci
                    PE.wait(dve, v_evac)
                    if s == 1 and tc == 0:
                        PE.wait(pe, v_read_pe)
                    for dc in range(8):
                        for half in range(2):
                            if dc == 7 and half == 1:
                                PE(lambda e, dc=dc, half=half, ci=ci: e.matmul(
                                    out=psC[:, 512 * half:512 * (half + 1)],
                                    lhsT=r2(uT3[:, dc:dc + 1, 128 * ci:128 * (ci + 1)]),
                                    rhs=wv_sb[:, dc * 1024 + 512 * half:dc * 1024 + 512 * (half + 1)],
                                    start=(dc == 0), stop=True).then_inc(pe.h, 1))
                            else:
                                PE(lambda e, dc=dc, half=half, ci=ci: e.matmul(
                                    out=psC[:, 512 * half:512 * (half + 1)],
                                    lhsT=r2(uT3[:, dc:dc + 1, 128 * ci:128 * (ci + 1)]),
                                    rhs=wv_sb[:, dc * 1024 + 512 * half:dc * 1024 + 512 * (half + 1)],
                                    start=(dc == 0), stop=(dc == 7)))
                    vmm = pe.bump()
                    DVE.wait(pe, vmm)
                    if s == 1 and tc == 0:
                        DVE.wait(pe, v_read_pe)
                    DVE(lambda e, tc=tc: e.tensor_copy(
                        out=vbuf[:, tc:tc + 1, :, 0:64].rearrange("p a h i -> p (a h) i"),
                        in_=psC[:, :].rearrange("p (h i) -> p h i", i=64)).then_inc(dve.h, 1))
                    v_evac = dve.bump()
                mm_prev_sb = pe.n
            qkv_xin_last = max(xin_free.get(14, 0), xin_free.get(15, 0))
            qw_stack.close()
            av_stack = ExitStack()
            aT2 = [[sb(f"aT{s}_{par}_{h}", [128, 16, 512], BF16, av_stack)
                    for h in range(2)] for par in range(2)]
            res_done = {}
            out_prev = 0
            sp_gate = [0, 0]
            av_done_pe = {}
            oev_by_pair = {}
            pend = None

            def emit_av_block(pd):
                nonlocal oev_prev
                st_, p_, expd, maskd, par = pd
                c_ = st_ * NPAIR + p_
                PE.wait(dve, v_evac)
                PE.wait(dve, oev_by_pair.get(c_ - 1, oev_prev))
                avz = 0
                for qt in range(4):
                    qt_g = 4 * st_ + qt
                    for h in range(2):
                        hg = 2 * p_ + h
                        PE.wait(act, expd[(h, qt_g // 2)])
                        PE.wait(dve, maskd[(h, qt)])
                        for kt2 in range(qt_g + 1):
                            if kt2 == qt_g:
                                PE(lambda e, h=h, qt=qt, kt2=kt2, hg=hg, par=par: e.matmul(
                                    out=psC[:, 512 * h + 65 * qt:512 * h + 65 * qt + 65],
                                    lhsT=r2(aT2[par][h][:, kt2:kt2 + 1, 128 * qt:128 * (qt + 1)]),
                                    rhs=vbuf[:, kt2:kt2 + 1, hg:hg + 1, :].rearrange("p a h i -> p (a h i)"),
                                    start=(kt2 == 0), stop=True).then_inc(pe.h, 1))
                            else:
                                PE(lambda e, h=h, qt=qt, kt2=kt2, hg=hg, par=par: e.matmul(
                                    out=psC[:, 512 * h + 65 * qt:512 * h + 65 * qt + 65],
                                    lhsT=r2(aT2[par][h][:, kt2:kt2 + 1, 128 * qt:128 * (qt + 1)]),
                                    rhs=vbuf[:, kt2:kt2 + 1, hg:hg + 1, :].rearrange("p a h i -> p (a h i)"),
                                    start=(kt2 == 0), stop=False))
                        avz = pe.bump()
                DVE.wait(pe, avz)
                for h in range(2):
                    for qt in range(4):
                        DVE(lambda e, h=h, qt=qt: e.reciprocal(
                            out=stat[0][:, 4 * h + qt:4 * h + qt + 1],
                            in_=psC[:, 512 * h + 65 * qt + 64:512 * h + 65 * qt + 65]))
                DVE(lambda e: e.nop().then_inc(dve.h, 1))
                rec_done = dve.bump()
                ACT.wait(dve, rec_done)
                ACT(lambda e: e.nop().then_inc(act.h, 1))
                rec_echo = act.bump()
                DVE.wait(act, rec_echo)
                for h in range(2):
                    hg = 2 * p_ + h
                    for qt in range(4):
                        DVE(lambda e, h=h, qt=qt, hg=hg: e.tensor_scalar(
                            out=r2(o_st[:, qt:qt + 1, 64 * hg:64 * (hg + 1)]),
                            in0=psC[:, 512 * h + 65 * qt:512 * h + 65 * qt + 64],
                            scalar1=stat[0][:, 4 * h + qt:4 * h + qt + 1],
                            scalar2=None, op0=ALU.mult,
                            op1=ALU.bypass).then_inc(dve.h, 1))
                        oev_prev = dve.bump()
                av_done_pe[c_] = pe.n
                oev_by_pair[c_] = oev_prev

            def emit_residual(st_):
                nonlocal out_prev
                DVE(lambda e: e.nop().then_inc(dve.h, 1))
                ob = dve.bump()
                ACT.wait(dve, ob)
                ACT(lambda e: e.nop().then_inc(act.h, 1))
                ob_echo = act.bump()
                DVE.wait(act, ob_echo)
                for qt in range(4):
                    gl = 4 * st_ + qt
                    g = 16 * s + gl
                    cbuf = gl % 2
                    gate = max(res_done.get(gl - 2, 0), qkv_xin_last)
                    SY.wait(dve, gate)
                    vin = dmain(SY, sx[cbuf], xin[cbuf][:, :],
                                xs1[128 * g:128 * (g + 1), :])
                    PO.wait(sx[cbuf], vin)
                    PO.wait(dmg, 16)
                    PO(lambda e, cbuf=cbuf: e.tensor_tensor(
                        out=xin[cbuf][:, :], in0=xin[cbuf][:, :], in1=bv_t[:, :],
                        op=ALU.add).then_inc(pool.h, 1))
                    pb_ = pool.bump()
                    DVE.wait(pool, pb_)
                    DVE.wait(dmo, out_prev)
                    DVE(lambda e, qt=qt, cbuf=cbuf: e.scalar_tensor_tensor(
                        out=xout[:, :], in0=r2(o_st[:, qt:qt + 1, :]), scalar=1.0,
                        in1=xin[cbuf][:, :], op0=ALU.mult,
                        op1=ALU.add).then_inc(dve.h, 1))
                    res_done[gl] = dve.bump()
                    SY.wait(dve, res_done[gl])
                    out_prev = dmain(SY, dmo, at_dst[128 * g:128 * (g + 1), :],
                                     xout[:, :])
                    if next_sums:
                        res_done[gl] = emit_sums(xout, g)

            for st in range(4):
                qlo = 512 * st
                for p in range(NPAIR):
                    c = st * NPAIR + p
                    par = c % 2
                    expd = {}
                    maskd = {}
                    nkt = 4 * st + 4
                    for k0 in range(0, nkt, 2):
                        for h in range(2):
                            S = psA if h == 0 else psB
                            pb = 64 * h
                            PE.wait(act, sp_gate[h])
                            if k0 == 0 and h == 0:
                                PE.wait(act, qk_done)
                                PE.wait(dmc, mv)
                            for ki in range(2):
                                kt = k0 + ki
                                qstart = max(qlo, 128 * kt)
                                nq = qlo + 512 - qstart
                                lo = 512 * ki + (qstart - qlo)
                                if ki == 1:
                                    PE(lambda e, S=S, lo=lo, pb=pb, p=p, kt=kt,
                                       qstart=qstart, qlo=qlo, nq=nq: e.matmul(
                                        out=S[:, lo:lo + nq],
                                        lhsT=r2(qkT3[pb:pb + 64, 8 + p:9 + p, 128 * kt:128 * (kt + 1)]),
                                        rhs=r2(qkT3[pb:pb + 64, p:p + 1, qstart:qlo + 512]),
                                        start=True, stop=True).then_inc(pe.h, 1))
                                else:
                                    PE(lambda e, S=S, lo=lo, pb=pb, p=p, kt=kt,
                                       qstart=qstart, qlo=qlo, nq=nq: e.matmul(
                                        out=S[:, lo:lo + nq],
                                        lhsT=r2(qkT3[pb:pb + 64, 8 + p:9 + p, 128 * kt:128 * (kt + 1)]),
                                        rhs=r2(qkT3[pb:pb + 64, p:p + 1, qstart:qlo + 512]),
                                        start=True, stop=True))
                            smm = pe.bump()
                            ACT.wait(pe, smm)
                            ACT.wait(pe, av_done_pe.get(c - 2, v_read_pe))
                            ACT(lambda e, h=h, k0=k0, S=S, par=par: e.activation(
                                out=r2(aT2[par][h][:, k0:k0 + 2, :]),
                                in_=S[:, 0:1024], func=AF.Exp, scale=0.125,
                                bias=EXP_SHIFT).then_inc(act.h, 1))
                            ev = act.bump()
                            sp_gate[h] = ev
                            expd[(h, k0 // 2)] = ev
                            if k0 >= 4 * st:
                                for ki in range(2):
                                    kt = k0 + ki
                                    qt = kt - 4 * st
                                    DVE.wait(act, ev)
                                    DVE(lambda e, h=h, kt=kt, qt=qt, par=par: e.tensor_tensor(
                                        out=r2(aT2[par][h][:, kt:kt + 1, 128 * qt:128 * (qt + 1)]),
                                        in0=r2(aT2[par][h][:, kt:kt + 1, 128 * qt:128 * (qt + 1)]),
                                        in1=mask_ut[:, :],
                                        op=ALU.mult).then_inc(dve.h, 1))
                                    maskd[(h, qt)] = dve.bump()
                    if pend is not None:
                        emit_av_block(pend)
                        if pend[1] == NPAIR - 1:
                            emit_residual(pend[0])
                    pend = (st, p, expd, maskd, par)
            emit_av_block(pend)
            emit_residual(3)
            pend = None
            qk_read_pe = pe.n
            v_read_pe = pe.n
            av_stack.close()
            xin_carry = [res_done.get(14, 0), res_done.get(15, 0)]
        at_stack.close()

    f1_dst = xs1 if len(RUN_PHASES) > 1 else y_out
    at_dst = xs2 if "f2" in RUN_PHASES else y_out
    if "f1" in RUN_PHASES:
        emit_ffn(x_in, f1_dst, wfc_sb, wp_sb, gT3_1, t1,
                 next_sums=("attn" in RUN_PHASES),
                 spill=("attn" in RUN_PHASES or "f2" in RUN_PHASES),
                 xin_seed=(prepass_rd[0], prepass_rd[1]),
                 w_gates={hf: ([(pool, ms)] + ([(act, thr_done_act)] if hf == 0 else []))
                          for hf, ms in wfc_jgate.items()},
                 wp_gate=(dve, prep_dve_last),
                 spill_gate=(pool, wp_pool_last),
                 g3b_gate=(pool, wp_pool_last))
        if len(RUN_PHASES) == 1:
            if DEBUG_TAPS:
                dmain(SY, dmc, dbg_gT2[:, 0:G3SPLIT * TSB], r2(gT3_1[0][:, :, :]))
                dmain(SY, dmc, dbg_wf, wfc_sb[:, :])
                dmain(SY, dmc, dbg_wp, wp_sb[:, :])
            SY.wait(dmo, dmo.n)
            SY.wait(sx[0], sx[0].n)
            SY.wait(sx[1], sx[1].n)
            SY.wait(sw[0], sw[0].n)
            SY.wait(sw[1], sw[1].n)
            SY.wait(sw[2], sw[2].n)
            SY.wait(dmw, dmw.n)
            SY.wait(dmc, dmc.n)
            SY.wait(dmg, dmg.n)
        flush_block()

    ffn_stack.close()

    if "attn" in RUN_PHASES:
        SY.wait(dmo, dmo.n)
        t2 = [emit_lntable(dve.n)]
        emit_attn(at_dst, t2, next_sums=("f2" in RUN_PHASES))
        if "f2" not in RUN_PHASES:
            SY.wait(dmo, dmo.n)
            SY.wait(sx[0], sx[0].n)
            SY.wait(sx[1], sx[1].n)
            SY.wait(sw[0], sw[0].n)
            SY.wait(sw[1], sw[1].n)
            SY.wait(sw[2], sw[2].n)
            SY.wait(dmw, dmw.n)
            SY.wait(dmc, dmc.n)
            SY.wait(dmg, dmg.n)
        flush_block()

    if "f2" in RUN_PHASES:
        ffn2 = ExitStack()
        wfc2 = sb("wfc_sb2", [128, 8 * HID], BF16, ffn2)
        wp2 = sb("wp_sb2", [128, 32 * D], BF16, ffn2)
        gT3_2 = (sb("gT3_2a", [128, G3SPLIT, TSB], BF16, ffn2),
                 sb("gT3_2b", [128, 32 - G3SPLIT, TSB], BF16, ffn2))
        w_gates = {}
        stripe_sems = [sw[0], sw[1], sw[2], dmg]
        for j in range(4):
            sj = stripe_sems[j]
            vj = dmain(SY, sj,
                       wfc2[:, :].rearrange("p (dc j n) -> p dc j n", dc=8, j=4)[:, :, j, :],
                       wspill[:, 0:32768].rearrange("p (dc j n) -> p dc j n", dc=8, j=4)[:, :, j, :])
            w_gates[8 * j] = (sj, vj)
        vr = dmain(SY, dmw, wp2[:, :], wspill[:, 32768:65536])
        SY.wait(dmo, dmo.n)
        t3 = [emit_lntable(dve.n)]
        emit_ffn(xs2, y_out, wfc2, wp2, gT3_2, t3, next_sums=False, spill=False,
                 w_gates=w_gates, wp_gate=(dmw, vr))
        SY.wait(dmo, dmo.n)
        SY.wait(sx[0], sx[0].n)
        SY.wait(sx[1], sx[1].n)
        SY.wait(sw[0], sw[0].n)
        SY.wait(sw[1], sw[1].n)
        SY.wait(sw[2], sw[2].n)
        SY.wait(dmw, dmw.n)
        SY.wait(dmc, dmc.n)
        SY.wait(dmg, dmg.n)
        flush_block()
        ffn2.close()

    es.close()
    return nc


def host_inputs(inputs):
    x = np.ascontiguousarray(inputs["L0_attn_hidden_states"], dtype=np.float32)
    wqk = np.ascontiguousarray(inputs["L0_attn_Wqk"], dtype=np.float32)
    bqk = np.ascontiguousarray(inputs["L0_attn_bqk"], dtype=np.float32)
    wv = np.ascontiguousarray(inputs["L0_attn_Wv"], dtype=np.float32)
    bv = np.ascontiguousarray(inputs["L0_attn_bv"], dtype=np.float32)
    wfc = np.ascontiguousarray(inputs["L0_ffn_W_fc"], dtype=np.float32)
    lnw = np.ascontiguousarray(inputs["L0_ffn_ln_w"], dtype=np.float32)
    lnb = np.ascontiguousarray(inputs["L0_ffn_ln_b"], dtype=np.float32)
    wp = np.ascontiguousarray(inputs["L0_ffn_W_proj"], dtype=np.float32)

    shared = {
        "wfcT": np.ascontiguousarray(wfc.T),
        "wpT": np.ascontiguousarray(wp.T),
        "wqk": wqk,
        "wv": wv,
        "bqk_pt": np.ascontiguousarray(bqk.reshape(16, 128).T),
        "lnw_pt": np.ascontiguousarray(lnw.reshape(32, 128).T),
        "lnb_pt": np.ascontiguousarray(lnb.reshape(32, 128).T),
        "bv_t": np.ascontiguousarray(np.broadcast_to(bv, (128, D))),
        "id_bf": np.eye(128, dtype=BFNP),
        "id_f32": np.eye(128, dtype=np.float32),
        "ones_col": np.ones((128, 1), dtype=BFNP),
        "ones_row": np.ones((1, 128), dtype=np.float32),
        "mask_ut": np.triu(np.ones((128, 128), dtype=np.float32)).astype(BFNP),
        "cst": np.broadcast_to(np.array(
            [EPS, EXP_SHIFT, 0.0, 1.0, float(np.log(0.5)), 0.5e-6, 0.0, 0.0],
            dtype=np.float32), (128, 8)).copy(),
    }
    maps = []
    for c in range(8):
        m = dict(shared)
        m["x_in"] = np.ascontiguousarray(x[2 * c:2 * c + 2].reshape(T, D))
        maps.append(m)
    return maps


def kernel(**inputs):
    maps = host_inputs(inputs)
    nc = build_program()
    res = run_bass_kernel_spmd(nc, maps, core_ids=list(range(8)))
    out = np.empty((16, SEQ, D), dtype=np.float32)
    for c in range(8):
        out[2 * c:2 * c + 2] = np.asarray(res.results[c]["y"]).reshape(NSEQ, SEQ, D)
    return out
